# version 22
# baseline (speedup 1.0000x reference)
import os
import numpy as np
import ml_dtypes
from contextlib import ExitStack

DEBUG_STAGE = int(os.environ.get("KMK_DEBUG_STAGE", "0"))
DEBUG_DUMP = int(os.environ.get("KMK_DEBUG_DUMP", "0"))

import concourse.bass as bass
import concourse.mybir as mybir
from concourse.bass_utils import run_bass_kernel_spmd

F32 = mybir.dt.float32
BF16 = mybir.dt.bfloat16
F16 = mybir.dt.float16
INT8 = mybir.dt.int8
AF = mybir.ActivationFunctionType
ALU = mybir.AluOpType
AX = mybir.AxisListType

D = 1024
E1 = 2048
I = 2816
B, S = 4, 4096
TOK = 2048
HALO = 7
TR = TOK + 2 * HALO
NT = 17
TPAD = NT * 128
RAG = TR - 16 * 128
NHC = D // 128
NIC = I // 128
KCONV = 15
QMAX = 127.0
C23 = 12582912.0
CLO = C23 - 128.0
CHI = C23 + 127.0
EPS = 1e-5
TCH = [(0, 512), (512, 512), (1024, 512), (1536, 512), (2048, RAG)]
TW = [128] * 16 + [RAG]

ENGS = ("sync", "gpsimd", "tensor", "vector", "scalar")


class Prog:
    def __init__(self, nc):
        self.nc = nc
        self.q = {e: [] for e in ENGS}
        self.cnt = {}
        self.sems = {}

    def reg_sem(self, name, handle):
        self.sems[name] = handle
        self.cnt[name] = 0

    def emit(self, eng, fn):
        self.q[eng].append(fn)

    def inc(self, eng, fn, sem, n=1):
        self.cnt[sem] += n
        self.q[eng].append(
            lambda e, fn=fn, s=sem, n=n: fn(e).then_inc(self.sems[s], n))
        return self.cnt[sem]

    def wait(self, eng, sem, v=None):
        if v is None:
            v = self.cnt[sem]
        assert v <= self.cnt[sem], (sem, v, self.cnt[sem])
        if v > 0:
            self.q[eng].append(lambda e, s=sem, v=v: e.wait_ge(self.sems[s], v))

    def run(self, eng_obj, eng_name):
        for fn in self.q[eng_name]:
            fn(eng_obj)


def _finish(nc, p, dma, x_res, out_h, dump=None):
    for eng in ("sync",):
        p.wait(eng, "VE")
        p.wait(eng, "SE")
        p.wait(eng, "PE")
        p.wait(eng, "GE")
    d_out = 0
    for k in range(NT):
        if k == 0:
            src = x_res[HALO:128, 0:D]
            dst = out_h[0:121, :]
        elif k == NT - 1:
            src = x_res[0:RAG - HALO, k * D:(k + 1) * D]
            dst = out_h[128 * k - HALO:TOK, :]
        else:
            src = x_res[0:128, k * D:(k + 1) * D]
            dst = out_h[128 * k - HALO:128 * (k + 1) - HALO, :]
        d_out = dma("dS", "sync", dst, src)
    if DEBUG_DUMP and dump is not None:
        sca, bc_, rowb, hmaxc, wgTsc_, wuTsc_, hsc_, sg1_, su1_, sga_ = dump
        d_out = dma("dS", "sync", out_h[100:101, 0:8], sca[0:1, :])
        d_out = dma("dS", "sync", out_h[101:102, 0:4], bc_[0:1, 0:4])
        d_out = dma("dS", "sync", out_h[102:103, 0:128], rowb[0:1, :])
        d_out = dma("dS", "sync", out_h[103:231, 0:110], hmaxc[:, :])
        p.wait("gpsimd", "dS", d_out)
        dg = dma("dG", "gpsimd", out_h[231:359, 0:1024], wgTsc_[0])
        dg = dma("dG", "gpsimd", out_h[359:487, 0:1024], wuTsc_[0])
        dg = dma("dG", "gpsimd", out_h[487:615, 0:1024], hsc_[0][:, 0:1024])
        dg = dma("dG", "gpsimd", out_h[615:743, 0:22], sg1_[:, :])
        dg = dma("dG", "gpsimd", out_h[743:871, 0:22], su1_[:, :])
        dg = dma("dG", "gpsimd", out_h[871:999, 0:4], bc_[:, :])
        dg = dma("dG", "gpsimd", out_h[999:1127, 0:22], sga_[:, :])
        p.wait("sync", "dG", dg)
    p.wait("sync", "dS", d_out)

    with nc.Block() as block:
        @block.sync
        def _(eng):
            p.run(eng, "sync")

        @block.gpsimd
        def _(eng):
            p.run(eng, "gpsimd")

        @block.tensor
        def _(eng):
            p.run(eng, "tensor")

        @block.vector
        def _(eng):
            p.run(eng, "vector")

        @block.scalar
        def _(eng):
            p.run(eng, "scalar")

    return nc


def build_nc():
    nc = bass.Bass("TRN2", target_bir_lowering=False,
                   detect_race_conditions=False)

    x_h = nc.dram_tensor("x", [TPAD, D], F32, kind="ExternalInput").ap()
    w1_h = nc.dram_tensor("w1", [D, E1], F32, kind="ExternalInput").ap()
    b1_h = nc.dram_tensor("b1", [E1], F32, kind="ExternalInput").ap()
    w2_h = nc.dram_tensor("w2", [D, D], F32, kind="ExternalInput").ap()
    b2_h = nc.dram_tensor("b2", [D], F32, kind="ExternalInput").ap()
    csw_h = nc.dram_tensor("csw", [16, KCONV], F32, kind="ExternalInput").ap()
    clip_h = nc.dram_tensor("clips", [1, 2], F32, kind="ExternalInput").ap()
    wg_h = nc.dram_tensor("wg", [I, D], F32, kind="ExternalInput").ap()
    wu_h = nc.dram_tensor("wu", [I, D], F32, kind="ExternalInput").ap()
    wd_h = nc.dram_tensor("wd", [D, I], F32, kind="ExternalInput").ap()
    eye_h = nc.dram_tensor("eyeI", [128, 128], BF16, kind="ExternalInput").ap()
    onesf_h = nc.dram_tensor("onesf", [1, 128], F32, kind="ExternalInput").ap()
    onesb_h = nc.dram_tensor("onesb", [1, 128], BF16, kind="ExternalInput").ap()
    sel_h = nc.dram_tensor("sel", [16, NHC * 128], BF16, kind="ExternalInput").ap()
    edge_h = nc.dram_tensor("edge", [128, 2], F32, kind="ExternalInput").ap()
    out_h = nc.dram_tensor("out", [TOK, D], F32, kind="ExternalOutput").ap()

    hsc = nc.dram_tensor("hsc", [NIC, 128, TR], F16, kind="Internal").ap()
    wgTsc = nc.dram_tensor("wgTsc", [NIC, 128, D], BF16, kind="Internal").ap()
    wuTsc = nc.dram_tensor("wuTsc", [NIC, 128, D], BF16, kind="Internal").ap()
    wdTsc = nc.dram_tensor("wdTsc", [128, NIC * D], BF16, kind="Internal").ap()
    ccin = [nc.dram_tensor(f"ccin{i}", [128], F32, kind="Internal").ap() for i in range(5)]
    bb = nc.dram_tensor("bb", [2], F32, kind="Internal").ap()
    ccout = [nc.dram_tensor(f"ccout{i}", [128], F32, kind="Internal").ap() for i in range(5)]

    p = Prog(nc)
    ctx = ExitStack()
    proctx = ExitStack()

    def sb(name, shape, dtype, st=None):
        return (st or ctx).enter_context(nc.sbuf_tensor(name, shape, dtype))

    for s in ("dS", "dG", "PE", "VE", "SE", "GE", "CC"):
        p.reg_sem(s, ctx.enter_context(nc.semaphore(s)))

    x_res = sb("x_res", [128, NT * D], F32)
    actT = sb("actT", [128, NHC * TR], BF16)
    eye = sb("eye_s", [128, 128], BF16)
    onesb = sb("onesb_s", [1, 128], BF16)
    edge = sb("edge_s", [128, 2], F32)
    b1c = sb("b1c", [128, 2 * NHC], F32)
    filt = sb("filt", [128, NHC * KCONV], F32)
    cswb = sb("cswb", [16, KCONV], F32)
    clip2 = sb("clip2", [1, 2], F32)
    sig2 = sb("sig2", [1, 2], F32)
    csmax = sb("csmax", [16, 1], F32)
    cssum = sb("cssum", [16, 1], F32)
    softw = sb("softw", [16, KCONV], F32)
    stat6 = sb("stat6", [128, 12], F32)
    mv = sb("mv", [128, 2 * NT], F32)
    rstd = sb("rstd", [128, NT], F32)
    nmr = sb("nmr", [128, NT], F32)
    rs1 = sb("rs1", [128, NT], F32)
    bia1 = sb("bia1", [128, NT], F32)
    ymaxc = sb("ymaxc", [128, NT], F32)
    hmaxc = sb("hmaxc", [128, NIC], F32)
    sga = sb("sga", [128, NIC], F32)
    sua = sb("sua", [128, NIC], F32)
    sg1 = sb("sg1", [128, NIC], F32)
    su1 = sb("su1", [128, NIC], F32)
    rowb = sb("rowb", [1, 128], F32)
    sca = sb("sca", [1, 8], F32)
    bc = sb("bc", [128, 4], F32)
    invq = sb("invq", [128, 1], F32)
    sdh = sb("sdh", [128, 1], F32)
    rmx = sb("rmx", [128, 1], F32)
    spc = sb("spc", [128, 1], F32)
    epsb = sb("epsb", [128, 1], F32)
    cmagb = sb("cmagb", [128, 1], F32)
    ncmagb = sb("ncmagb", [128, 1], F32)
    xmx = sb("xmx", [128, NT], F32)
    xmn = sb("xmn", [128, NT], F32)
    r2t = [sb(f"r2t{i}", [128, D], F32) for i in range(2)]
    qtm = [sb(f"qtm{i}", [128, D], BF16) for i in range(2)]
    qint = [sb("qint0", [128, D], INT8)] * 2
    silu_t = [sb(f"sil{i}", [128, 512], F32) for i in range(2)]

    ps_tr = ctx.enter_context(nc.psum_tensor("ps_tr", [128, 1024], BF16))
    ps_a = [ctx.enter_context(nc.psum_tensor(f"ps_a{i}", [128, 512], F32)) for i in range(2)]
    ps_b = [ctx.enter_context(nc.psum_tensor(f"ps_b{i}", [128, 512], F32)) for i in range(2)]
    ps_c = [ctx.enter_context(nc.psum_tensor(f"ps_c{i}", [128, 512], F32)) for i in range(2)]
    ps_m = ctx.enter_context(nc.psum_tensor("ps_m", [128, 512], F32))

    actT3 = actT[:, :].rearrange("p (c t) -> p c t", c=NHC)

    def ps_tr_view(w):
        return ps_tr[:, :].rearrange("p (c x) -> p c x", c=NHC)[:, :, 0:w]

    def x_tile(k):
        return x_res[0:TW[k], k * D:(k + 1) * D]

    def dma(sem, eng, out, in_):
        return p.inc(eng, lambda e, o=out, i=in_: e.dma_start(out=o, in_=i), sem, 16)

    def phase_barrier():
        marks = {}
        marks["PE"] = p.inc("tensor", lambda e: e.matmul(
            ps_m[0:1, 0:1], eye[0:1, 0:1], eye[0:1, 0:1], start=True, stop=True), "PE")
        marks["VE"] = p.inc("vector", lambda e: e.engine_nop(), "VE")
        marks["SE"] = p.inc("scalar", lambda e: e.copy(sca[0:1, 7:8], epsb[0:1, 0:1]), "SE")
        marks["GE"] = p.inc("gpsimd", lambda e: e.engine_nop(), "GE")
        for eng in ENGS:
            for s in ("PE", "VE", "SE", "GE"):
                if (eng, s) in (("tensor", "PE"), ("vector", "VE"),
                                ("scalar", "SE"), ("gpsimd", "GE")):
                    continue
                p.wait(eng, s, marks[s])
        for eng in ("tensor", "vector", "scalar", "gpsimd"):
            p.wait(eng, "dS")
            p.wait(eng, "dG")

    def phase_waits():
        for eng in ("sync", "gpsimd"):
            for s in ("PE", "VE", "SE"):
                p.wait(eng, s)

    def vsp(n=2):
        for _ in range(n):
            p.emit("vector", lambda e: e.tensor_copy(spc[:, :], epsb[:, :]))

    xh3 = x_h.rearrange("(n p) d -> p n d", p=128)
    xr3 = x_res[:, :].rearrange("p (n d) -> p n d", n=NT)
    dxh = dma("dS", "sync", xr3[:, 0:9, :], xh3[:, 0:9, :])
    dx = dma("dG", "gpsimd", xr3[:, 9:NT, :], xh3[:, 9:NT, :])
    dma("dS", "sync", eye[:, :], eye_h)
    dma("dS", "sync", onesb[0:1, :], onesb_h)
    dma("dS", "sync", edge[:, :], edge_h)
    p.inc("sync", lambda e: e.dma_start(
        out=b1c[:, :], in_=b1_h.rearrange("(n p) -> p n", p=128),
        allow_slow_non_contiguous=True), "dS", 16)
    selb = sb("selb", [16, NHC * 128], BF16, proctx)
    softwb = sb("softwb", [16, KCONV], BF16, proctx)
    dma("dS", "sync", selb[:, :], sel_h)
    dma("dS", "sync", cswb[:, :], csw_h)
    dcl = dma("dS", "sync", clip2[0:1, :], clip_h)
    dwu = dma("dS", "sync", ccin[4].rearrange("(p one) -> p one", one=1),
              edge[:, 0:1])
    p.wait("gpsimd", "dS", dwu)
    p.inc("gpsimd", lambda e: e.collective_compute(
        "AllReduce", ALU.max, replica_groups=[list(range(8))],
        ins=[ccin[4].opt()], outs=[ccout[4].opt()]), "CC")

    p.emit("vector", lambda e: e.memset(mv[:, 2 * (NT - 1):2 * NT], 0.0))
    p.emit("vector", lambda e: e.memset(ymaxc[:, NT - 1:NT], 0.0))
    p.emit("vector", lambda e: e.memset(epsb[:, :], EPS))
    p.emit("vector", lambda e: e.memset(ncmagb[:, :], -C23))
    p.emit("vector", lambda e: e.memset(xmx[:, :], 0.0))
    p.emit("vector", lambda e: e.memset(xmn[:, :], 0.0))
    p.inc("vector", lambda e: e.memset(cmagb[:, :], C23), "VE")
    p.wait("scalar", "VE")
    p.wait("scalar", "dS", dcl)
    p.inc("scalar", lambda e: e.activation(sig2[0:1, :], clip2[0:1, :], AF.Sigmoid), "SE")

    p.wait("vector", "dS", dcl)
    p.emit("vector", lambda e: e.tensor_reduce(csmax[:, :], cswb[:, :], AX.X, ALU.max))
    p.emit("vector", lambda e: e.tensor_copy(sca[0:1, 6:7], epsb[0:1, 0:1]))
    p.emit("vector", lambda e: e.tensor_copy(sca[0:1, 7:8], epsb[0:1, 0:1]))
    ve = p.inc("vector", lambda e: e.tensor_scalar_mul(csmax[:, :], csmax[:, :], -1.0), "VE")
    p.wait("scalar", "VE", ve)
    se = p.inc("scalar", lambda e: e.activation(softw[:, :], cswb[:, :], AF.Exp, bias=csmax[:, :]), "SE")
    p.wait("vector", "SE", se)
    p.emit("vector", lambda e: e.tensor_reduce(cssum[:, :], softw[:, :], AX.X, ALU.add))
    p.emit("vector", lambda e: e.tensor_copy(sca[0:1, 6:7], epsb[0:1, 0:1]))
    p.emit("vector", lambda e: e.tensor_copy(sca[0:1, 7:8], epsb[0:1, 0:1]))
    p.emit("vector", lambda e: e.reciprocal(cssum[:, :], cssum[:, :]))
    p.emit("vector", lambda e: e.tensor_copy(sca[0:1, 6:7], epsb[0:1, 0:1]))
    p.emit("vector", lambda e: e.tensor_copy(sca[0:1, 7:8], epsb[0:1, 0:1]))
    p.emit("vector", lambda e: e.tensor_scalar_mul(softw[:, :], softw[:, :], cssum[:, :]))
    vsp()
    ve_sw = p.inc("vector", lambda e: e.tensor_copy(softwb[:, :], softw[:, :]), "VE")
    p.wait("tensor", "VE", ve_sw)
    p.wait("tensor", "dS")
    for dc in range(NHC):
        p.wait("tensor", "VE")
        pe = p.inc("tensor", lambda e, dc=dc: e.matmul(
            ps_m[:, 0:KCONV], selb[:, dc * 128:(dc + 1) * 128], softwb[:, :],
            start=True, stop=True), "PE")
        p.wait("vector", "PE", pe)
        p.inc("vector", lambda e, dc=dc: e.tensor_copy(
            filt[:, dc * KCONV:(dc + 1) * KCONV], ps_m[:, 0:KCONV]), "VE")

    proctx.close()

    def emit_bn(k, w):
        p.emit("vector", lambda e, k=k, w=w: e.bn_stats(
            stat6[0:w, 0:6], x_res[0:w, k * D:k * D + 512]))
        p.emit("vector", lambda e, k=k, w=w: e.bn_stats(
            stat6[0:w, 6:12], x_res[0:w, k * D + 512:(k + 1) * D]))
        vsp()
        p.emit("vector", lambda e, k=k, w=w: e.bn_aggr(
            mv[0:w, 2 * k:2 * k + 2], stat6[0:w, :]))

    def emit_ln():
        p.wait("vector", "dS", dxh)
        p.wait("vector", "SE")
        for k in range(NT):
            if k == 9:
                p.wait("vector", "dG", dx)
            emit_bn(k, TW[k])
        return emit_ln_tail()

    def emit_ln_tail():
        vlast = p.inc("vector", lambda e: e.engine_nop(), "VE")
        p.wait("scalar", "VE", vlast)
        var_view = mv[:, :].rearrange("p (k two) -> p k two", two=2)[:, :, 1]
        slast = p.inc("scalar", lambda e: e.activation(
            rstd[:, :], var_view, AF.Sqrt, bias=epsb[:, :]), "SE")
        p.wait("vector", "SE", slast)
        p.emit("vector", lambda e: e.reciprocal(rstd[:, :], rstd[:, :]))
        vsp()
        mean_view = mv[:, :].rearrange("p (k two) -> p k two", two=2)[:, :, 0]
        p.emit("vector", lambda e: e.tensor_tensor(nmr[:, :], mean_view, rstd[:, :], ALU.mult))
        vsp()
        return p.inc("vector", lambda e: e.tensor_scalar_mul(nmr[:, :], nmr[:, :], -1.0), "VE")

    def cross_core_max(idx, local_col):
        d1 = dma("dS", "sync", ccin[idx].rearrange("(p one) -> p one", one=1), local_col)
        p.wait("gpsimd", "dS", d1)
        cc = p.inc("gpsimd", lambda e, i=idx: e.collective_compute(
            "AllReduce", ALU.max, replica_groups=[list(range(8))],
            ins=[ccin[i].opt()], outs=[ccout[i].opt()]), "CC")
        p.wait("sync", "CC", cc)
        d2 = dma("dS", "sync", rowb[0:1, :], ccout[idx][None, :])
        p.wait("vector", "dS", d2)
        return p.inc("vector", lambda e, i=idx: e.tensor_reduce(
            sca[0:1, i:i + 1], rowb[0:1, :], AX.X, ALU.max), "VE")

    def make_scales(idx, sig_col, s_col, gve):
        p.wait("vector", "VE", gve)
        p.emit("vector", lambda e, i=idx, sc=sig_col: e.tensor_scalar(
            sca[0:1, 4:5], sca[0:1, i:i + 1], sig2[0:1, sc:sc + 1],
            1.0 / QMAX, ALU.mult, ALU.mult))
        p.emit("vector", lambda e: e.tensor_copy(sca[0:1, 6:7], epsb[0:1, 0:1]))
        p.emit("vector", lambda e: e.tensor_copy(sca[0:1, 7:8], epsb[0:1, 0:1]))
        ve = p.inc("vector", lambda e: e.reciprocal(sca[0:1, 5:6], sca[0:1, 4:5]), "VE")
        p.wait("sync", "VE", ve)
        dma("dS", "sync", bb[None, :], sca[0:1, 4:6])
        d2 = dma("dS", "sync", bc[:, s_col:s_col + 2],
                 bb[None, :].partition_broadcast(128))
        p.wait("vector", "dS", d2)
        return p.inc("vector", lambda e: e.engine_nop(), "VE")

    def emit_ymax_and_barrier(idx, sig_col, s_col):
        re_of = [p.cnt["VE"], p.cnt["VE"]]
        for k in range(NT):
            w = TW[k]
            b = k % 2
            p.wait("scalar", "VE", re_of[b])
            se = p.inc("scalar", lambda e, k=k, w=w, b=b: e.activation(
                r2t[b][0:w, :], x_tile(k), AF.Identity,
                bias=nmr[0:w, k:k + 1], scale=rstd[0:w, k:k + 1]), "SE")
            p.wait("vector", "SE", se)
            re_of[b] = p.inc("vector", lambda e, k=k, w=w, b=b: e.tensor_reduce(
                ymaxc[0:w, k:k + 1], r2t[b][0:w, :], AX.X, ALU.max,
                apply_absolute_value=True), "VE")
        p.wait("vector", "dS")
        ve = p.inc("vector", lambda e: e.tensor_reduce(
            rmx[:, :], ymaxc[:, :], AX.X, ALU.max), "VE")
        p.wait("sync", "VE", ve)
        gve = cross_core_max(idx, rmx[:, 0:1])
        return make_scales(idx, sig_col, s_col, gve)

    def emit_quant_lnq(s_col):
        vsp()
        p.emit("vector", lambda e, c=s_col: e.tensor_scalar_mul(
            rs1[:, :], rstd[:, :], bc[:, c + 1:c + 2]))
        p.emit("vector", lambda e, c=s_col: e.tensor_scalar_mul(
            bia1[:, :], nmr[:, :], bc[:, c + 1:c + 2]))
        vsp()
        v0 = p.inc("vector", lambda e: e.engine_nop(), "VE")
        pe_of = [0, 0]
        r2_of = [v0, v0]
        for k in range(NT):
            w = TW[k]
            b = k % 2
            p.wait("scalar", "VE", r2_of[b])
            se = p.inc("scalar", lambda e, k=k, w=w, b=b: e.activation(
                r2t[b][0:w, :], x_tile(k), AF.Identity,
                bias=bia1[0:w, k:k + 1], scale=rs1[0:w, k:k + 1]), "SE")
            p.wait("vector", "SE", se)
            p.wait("vector", "PE", pe_of[b])
            p.emit("vector", lambda e, w=w, b=b: e.tensor_scalar(
                r2t[b][0:w, :], r2t[b][0:w, :], C23, CLO, ALU.add, ALU.max))
            p.emit("vector", lambda e, w=w, b=b: e.tensor_scalar(
                qint[b][0:w, :], r2t[b][0:w, :], CHI, -C23, ALU.min, ALU.add))
            vq = p.inc("vector", lambda e, w=w, b=b: e.tensor_copy(
                qtm[b][0:w, :], qint[b][0:w, :]), "VE")
            r2_of[b] = vq
            p.wait("tensor", "VE", vq)
            p.wait("tensor", "SE")
            for hc in range(NHC):
                fn = lambda e, hc=hc, w=w, b=b: e.transpose(
                    ps_tr[:, hc * 128:hc * 128 + w],
                    qtm[b][0:w, hc * 128:(hc + 1) * 128], eye[0:w, 0:w])
                if hc == NHC - 1:
                    pe = p.inc("tensor", fn, "PE")
                else:
                    p.emit("tensor", fn)
            pe_of[b] = pe
            p.wait("scalar", "PE", pe)
            p.inc("scalar", lambda e, k=k, w=w: e.copy(
                actT3[:, :, k * 128:k * 128 + w], ps_tr_view(w)), "SE")
        return p.cnt["SE"]

    def emit_wq_quant(ic, b, lgb, lub, state):
        outs = []
        for wi, (land, sc_dst) in enumerate(((lgb, sga), (lub, sua))):
            p.wait("vector", "dS", state["dma"][ic][wi])
            p.wait("vector", "SE")
            p.emit("vector", lambda e, land=land: e.tensor_reduce(
                rmx[:, :], land[:, :], AX.X, ALU.max, apply_absolute_value=True))
            vsp()
            p.emit("vector", lambda e, sc_dst=sc_dst, ic=ic: e.tensor_scalar_mul(
                sc_dst[:, ic:ic + 1], rmx[:, :], 1.0 / QMAX))
            p.emit("vector", lambda e: e.reciprocal(invq[:, :], rmx[:, :]))
            vsp()
            ve = p.inc("vector", lambda e: e.tensor_scalar_mul(
                invq[:, :], invq[:, :], QMAX), "VE")
            p.wait("scalar", "VE", ve)
            p.emit("scalar", lambda e, land=land: e.activation(
                land[:, :], land[:, :], AF.Identity, bias=cmagb[:, :], scale=invq[:, :]))
            qb = state["qb"]
            wq_b = state["wq_bft"][qb]
            p.wait("scalar", "PE", state["bft_pe"][qb])
            p.emit("scalar", lambda e, land=land, qb=qb: e.activation(
                qint[qb][:, :], land[:, :], AF.Identity, bias=ncmagb[:, :]))
            ge = p.inc("scalar", lambda e, wq_b=wq_b, qb=qb: e.copy(
                wq_b[:, :], qint[qb][:, :]), "SE")
            state["qb"] = 1 - qb
            outs.append((qb, ge))
        state["land_free"][b] = outs[1][1]
        state["quant"][ic] = outs

    def emit_wq_transpose(ic, b, stages, state):
        ready = []
        for wi, scratch in enumerate((wgTsc, wuTsc)):
            qb, ge = state["quant"][ic][wi]
            wq_b = state["wq_bft"][qb]
            p.wait("tensor", "SE", ge)
            p.wait("tensor", "SE")
            for hc in range(NHC):
                fn = lambda e, hc=hc, wq_b=wq_b: e.transpose(
                    ps_tr[:, hc * 128:(hc + 1) * 128],
                    wq_b[:, hc * 128:(hc + 1) * 128], eye[:, :])
                if hc == NHC - 1:
                    pe = p.inc("tensor", fn, "PE")
                else:
                    p.emit("tensor", fn)
            state["bft_pe"][qb] = pe
            stage = stages[wi][b]
            p.wait("scalar", "PE", state["stage_pe"][wi][b])
            p.wait("scalar", "dS", state["stage_dma"][wi][b])
            p.wait("scalar", "PE", pe)
            se2 = p.inc("scalar", lambda e, stage=stage: e.copy(stage[:, :], ps_tr[:, :]), "SE")
            p.wait("sync", "SE", se2)
            state["stage_dma"][wi][b] = dma("dS", "sync", scratch[ic], stage[:, :])
            ready.append(se2)
        state["ready"][ic] = (ready[0], ready[1], None)

    def emit_gateup(call, se_actT_done, bufs, s_col, wd_hook=None, wd_post=None,
                    hb_hook=None):
        lgb, lub = bufs.get("lg"), bufs.get("lu")
        h_stage = bufs["h_stage"]
        stages = (bufs["wgT"], bufs["wuT"])
        state = {
            "qb": 0,
            "wq_bft": bufs.get("wq_bft"),
            "bft_pe": [0, 0],
            "stage_pe": [[0, 0], [0, 0]],
            "stage_dma": [[0, 0], [0, 0]],
            "land_free": [0, 0],
            "dma": {},
            "ready": {},
            "quant": {},
        }
        evac_ve = [0, 0]
        silu_se = [0, 0]
        hstore = [0, 0]
        prev_mm = 0
        if call == 1:
            p.wait("vector", "VE")
            p.emit("vector", lambda e, c=s_col: e.tensor_scalar_mul(
                sg1[:, :], sga[:, :], bc[:, c:c + 1]))
            p.inc("vector", lambda e, c=s_col: e.tensor_scalar_mul(
                su1[:, :], sua[:, :], bc[:, c:c + 1]), "VE")

        def weight_quant(ic):
            b = ic % 2
            if call == 0:
                if ic >= 2:
                    p.wait("gpsimd", "SE", state["land_free"][b])
                d0 = dma("dG", "gpsimd", lgb[b][:, :], wg_h[ic * 128:(ic + 1) * 128, :])
                d1 = dma("dG", "gpsimd", lub[b][:, :], wu_h[ic * 128:(ic + 1) * 128, :])
                state["dma"][ic] = (d0, d1)
                emit_wq_quant(ic, b, lgb[b], lub[b], state)
                p.emit("vector", lambda e, ic=ic, c=s_col: e.tensor_scalar_mul(
                    sg1[:, ic:ic + 1], sga[:, ic:ic + 1], bc[:, c:c + 1]))
                p.inc("vector", lambda e, ic=ic, c=s_col: e.tensor_scalar_mul(
                    su1[:, ic:ic + 1], sua[:, ic:ic + 1], bc[:, c:c + 1]), "VE")
            else:
                if ic >= 2:
                    p.wait("sync", "PE", state["stage_pe"][1][b])
                dma("dS", "sync", stages[0][b][:, :], wgTsc[ic])
                du = dma("dS", "sync", stages[1][b][:, :], wuTsc[ic])
                state["ready"][ic] = (None, None, du)

        def weight_fin(ic):
            if call == 0:
                emit_wq_transpose(ic, ic % 2, stages, state)

        state["quant"] = {}
        weight_quant(0)
        weight_fin(0)
        weight_quant(1)
        for ic in range(NIC):
            b = ic % 2
            if ic + 1 < NIC:
                weight_fin(ic + 1)
            if call == 0 and ic + 2 < NIC:
                weight_quant(ic + 2)
            if wd_hook is not None:
                wd_hook(ic)
            rg, ru, du = state["ready"][ic]
            if call == 0:
                p.wait("tensor", "SE", rg)
                p.wait("tensor", "SE", ru)
            else:
                p.wait("tensor", "dS", du)
            p.wait("tensor", "SE", se_actT_done)
            for it, (t0, tw_) in enumerate(TCH):
                pb = it % 2
                p.wait("tensor", "VE", evac_ve[pb])
                p.wait("tensor", "SE", silu_se[pb])
                for pi, (pst, wst) in enumerate(((ps_a, stages[0][b]), (ps_b, stages[1][b]))):
                    for hc in range(NHC):
                        fn = lambda e, hc=hc, t0=t0, tw_=tw_, pst=pst, wst=wst, pb=pb: e.matmul(
                            pst[pb][:, 0:tw_], wst[:, hc * 128:(hc + 1) * 128],
                            actT[:, hc * TR + t0:hc * TR + t0 + tw_],
                            start=(hc == 0), stop=(hc == NHC - 1))
                        if pi == 1 and hc == NHC - 1:
                            prev_mm = p.inc("tensor", fn, "PE")
                        else:
                            p.emit("tensor", fn)
                p.wait("scalar", "PE", prev_mm)
                p.wait("scalar", "VE")
                silu_se[pb] = p.inc("scalar", lambda e, ic=ic, tw_=tw_, pb=pb: e.activation(
                    silu_t[pb][:, 0:tw_], ps_a[pb][:, 0:tw_], AF.Silu,
                    scale=sg1[:, ic:ic + 1]), "SE")
                hb = ic % 2
                p.wait("vector", "SE", silu_se[pb])
                p.wait("vector", "PE", prev_mm)
                if it == 0:
                    p.wait("vector", "dS", hstore[hb])
                evac_ve[pb] = p.inc("vector", lambda e, ic=ic, t0=t0, tw_=tw_, pb=pb, hb=hb: e.scalar_tensor_tensor(
                    h_stage[hb][:, t0:t0 + tw_], ps_b[pb][:, 0:tw_], su1[:, ic:ic + 1],
                    silu_t[pb][:, 0:tw_], ALU.mult, ALU.mult), "VE")
            vsp()
            p.emit("vector", lambda e, ic=ic, hb=hb: e.tensor_reduce(
                hmaxc[:, ic:ic + 1], h_stage[hb][:, 0:TR],
                AX.X, ALU.max, apply_absolute_value=True))
            state["stage_pe"][0][b] = prev_mm
            state["stage_pe"][1][b] = prev_mm
            if call == 1 and ic + 2 < NIC:
                weight_quant(ic + 2)
            if hb_hook is not None and ic == NIC - 1:
                hb_hook()
            if wd_post is not None:
                wd_post(ic)
            p.wait("sync", "VE", evac_ve[(len(TCH) - 1) % 2])
            p.wait("sync", "VE")
            hstore[ic % 2] = dma("dS", "sync", hsc[ic], h_stage[ic % 2][:, :])
        return prev_mm

    def emit_hbarrier_start(idx):
        p.wait("vector", "dS")
        vsp()
        ve = p.inc("vector", lambda e: e.tensor_reduce(
            rmx[:, :], hmaxc[:, :], AX.X, ALU.max), "VE")
        p.wait("sync", "VE", ve)
        d1 = dma("dS", "sync", ccin[idx].rearrange("(p one) -> p one", one=1),
                 rmx[:, 0:1])
        p.wait("gpsimd", "dS", d1)
        return p.inc("gpsimd", lambda e, i=idx: e.collective_compute(
            "AllReduce", ALU.max, replica_groups=[list(range(8))],
            ins=[ccin[i].opt()], outs=[ccout[i].opt()]), "CC")

    def emit_hbarrier_fin(idx, s_col, cc):
        p.wait("sync", "CC", cc)
        d2 = dma("dS", "sync", rowb[0:1, :], ccout[idx][None, :])
        p.wait("vector", "dS", d2)
        gve = p.inc("vector", lambda e, i=idx: e.tensor_reduce(
            sca[0:1, i:i + 1], rowb[0:1, :], AX.X, ALU.max), "VE")
        return make_scales(idx, 1, s_col, gve)

    def down_gather(hqr2b, k, w):
        hv = hqr2b[:, :].rearrange("p (c t) -> p c t", c=NIC)
        NH1 = NIC // 2
        return [dma("dG", "gpsimd", hv[:, h * NH1:(h + 1) * NH1, 0:w],
                    hsc[h * NH1:(h + 1) * NH1, :, k * 128:k * 128 + w]
                    .rearrange("c p t -> p c t")) for h in range(2)]

    def emit_down(s_col, ve_sc, wdT, bufs, bn_hook=None, pre_g=None, wdTb=0):
        hq_bf, hqr2, hqi8 = bufs["hq_bf"], bufs["hqr2"], bufs["hqi8"]
        p.wait("gpsimd", "dS")
        p.wait("vector", "VE", ve_sc)
        p.wait("scalar", "VE", ve_sc)
        pe_down = [0, 0]
        vq_of = [0, 0]
        vprev = [0, 0]
        NH1 = NIC // 2
        for k in range(NT):
            w = TW[k]
            b = k % 2
            if k == 0 and pre_g is not None:
                dkh = pre_g
            else:
                if k >= 2:
                    p.wait("gpsimd", "VE", vq_of[b])
                dkh = down_gather(hqr2[b], k, w)
            vqh = [0, 0]
            for h in range(2):
                c0 = h * NH1 * 128
                cw = NH1 * 128
                p.wait("scalar", "dG", dkh[h])
                seh = p.inc("scalar", lambda e, b=b, c0=c0, cw=cw: e.activation(
                    hqr2[b][:, c0:c0 + cw], hqr2[b][:, c0:c0 + cw], AF.Identity,
                    bias=cmagb[:, :], scale=bc[:, s_col + 1:s_col + 2]), "SE")
                p.wait("vector", "SE", seh)
                if h == 0:
                    p.wait("vector", "PE", pe_down[b])
                p.emit("vector", lambda e, b=b, c0=c0, cw=cw: e.tensor_scalar(
                    hqr2[b][:, c0:c0 + cw], hqr2[b][:, c0:c0 + cw],
                    CLO, CHI, ALU.max, ALU.min))
                p.emit("vector", lambda e, b=b, c0=c0, cw=cw: e.tensor_single_scalar(
                    hqi8[b][:, c0:c0 + cw], hqr2[b][:, c0:c0 + cw], -C23, ALU.add))
                vqh[h] = p.inc("vector", lambda e, b=b, c0=c0, cw=cw: e.tensor_copy(
                    hq_bf[b][:, c0:c0 + cw], hqi8[b][:, c0:c0 + cw]), "VE")
            vq_of[b] = vqh[1]
            p.wait("tensor", "VE", vqh[0])
            p.wait("tensor", "VE", vprev[0])
            p.wait("tensor", "VE", vprev[1])
            for e2 in range(2):
                for ic in range(NIC):
                    if k == 0 and e2 == 0 and ic == NH1 and wdTb:
                        p.wait("tensor", "dG", wdTb)
                    if e2 == 0 and ic == NH1:
                        p.wait("tensor", "VE", vqh[1])
                    fn = lambda e, ic=ic, e2=e2, b=b, w=w: e.matmul(
                        ps_c[e2][0:w, 0:512], hq_bf[b][:, ic * 128:ic * 128 + w],
                        wdT[:, ic * D + e2 * 512:ic * D + e2 * 512 + 512],
                        start=(ic == 0), stop=(ic == NIC - 1))
                    if e2 == 1 and ic == NIC - 1:
                        pe_down[b] = p.inc("tensor", fn, "PE")
                    else:
                        p.emit("tensor", fn)
            p.wait("vector", "PE", pe_down[b])
            for e2 in range(2):
                vprev[e2] = p.inc("vector", lambda e, e2=e2, k=k, w=w: e.scalar_tensor_tensor(
                    x_res[0:w, k * D + e2 * 512:k * D + e2 * 512 + 512],
                    ps_c[e2][0:w, 0:512], bc[0:w, s_col:s_col + 1],
                    x_res[0:w, k * D + e2 * 512:k * D + e2 * 512 + 512],
                    ALU.mult, ALU.add), "VE")
            if bn_hook is not None:
                bn_hook(k, w)
        return p.inc("vector", lambda e: e.engine_nop(), "VE")

    if DEBUG_STAGE == 10:
        return _finish(nc, p, dma, x_res, out_h, dump=(sca, bc, rowb, hmaxc, wgTsc, wuTsc, hsc, sg1, su1, sga))
    emit_ln()
    if DEBUG_STAGE == 11:
        return _finish(nc, p, dma, x_res, out_h, dump=(sca, bc, rowb, hmaxc, wgTsc, wuTsc, hsc, sg1, su1, sga))
    sc1 = emit_ymax_and_barrier(0, 0, 0)
    if DEBUG_STAGE == 12:
        return _finish(nc, p, dma, x_res, out_h, dump=(sca, bc, rowb, hmaxc, wgTsc, wuTsc, hsc, sg1, su1, sga))
    p.wait("vector", "VE", sc1)
    se_actT = emit_quant_lnq(0)
    if DEBUG_STAGE == 13:
        return _finish(nc, p, dma, x_res, out_h, dump=(sca, bc, rowb, hmaxc, wgTsc, wuTsc, hsc, sg1, su1, sga))

    s1ctx = ExitStack()
    bufs1 = {
        "lg": [sb(f"lg{i}", [128, D], F32, s1ctx) for i in range(2)],
        "lu": [sb(f"lu{i}", [128, D], F32, s1ctx) for i in range(2)],
        "wgT": [sb(f"wgT{i}", [128, D], BF16, s1ctx) for i in range(2)],
        "wuT": [sb(f"wuT{i}", [128, D], BF16, s1ctx) for i in range(2)],
        "wq_bft": [sb(f"wq_bft{i}", [128, D], BF16, s1ctx) for i in range(2)],
    }
    wdct = ExitStack()
    wd_land = [[sb(f"wdl{b}_{i}", [128, I // 2], F32, wdct) for i in range(2)]
               for b in range(2)]
    wdinvq = [sb(f"wdinvq{b}", [128, 1], F32, wdct) for b in range(2)]
    sdh2 = [sb(f"sdh2_{b}", [128, 1], F32, wdct) for b in range(2)]
    wdi8 = sb("wdi8", [128, I // 4], INT8, wdct)
    wd_q = [sb(f"wd_q{i}", [128, I], BF16, wdct) for i in range(2)]
    wd_stage = sb("wd_stage", [128, (NIC // 2) * 128], BF16, wdct)
    HI = I // 2
    HQ4 = I // 4
    wd_state = {"ge_b": [0, 0], "pe": 0, "pe_b": [0, 0], "sdma": 0,
                "halfq": [0, 0], "ve127_b": [0, 0], "act_se": [0, 0]}
    hstctx = ExitStack()
    bufs1["h_stage"] = [sb(f"hst{i}", [128, TR], F16, hstctx) for i in range(2)]
    wdTsc3 = wdTsc[:, :].rearrange("p (c d) -> p c d", c=NIC)

    def emit_wd_part1(hc):
        qb = hc % 2
        land = wd_land[qb]
        p.wait("gpsimd", "VE", wd_state["ge_b"][qb])
        d0 = dma("dG", "gpsimd", land[0][:, :], wd_h[hc * 128:(hc + 1) * 128, 0:HI])
        d1 = dma("dG", "gpsimd", land[1][:, :], wd_h[hc * 128:(hc + 1) * 128, HI:I])
        p.wait("vector", "dG", d1)
        p.wait("vector", "SE", wd_state["act_se"][qb])
        p.emit("vector", lambda e, land=land: e.tensor_reduce(
            rmx[:, :], land[0][:, :], AX.X, ALU.max, apply_absolute_value=True))
        p.emit("vector", lambda e, land=land, qb=qb: e.tensor_reduce(
            wdinvq[qb][:, :], land[1][:, :], AX.X, ALU.max, apply_absolute_value=True))
        vsp()
        p.emit("vector", lambda e, qb=qb: e.tensor_tensor(
            rmx[:, :], rmx[:, :], wdinvq[qb][:, :], ALU.max))
        vsp()
        p.emit("vector", lambda e, qb=qb: e.tensor_scalar_mul(
            sdh2[qb][:, :], rmx[:, :], 0.5 / QMAX))
        p.emit("vector", lambda e, qb=qb: e.reciprocal(wdinvq[qb][:, :], rmx[:, :]))
        vsp()
        wd_state["ve127_b"][qb] = p.inc("vector", lambda e, qb=qb: e.tensor_scalar_mul(
            wdinvq[qb][:, :], wdinvq[qb][:, :], QMAX), "VE")

    def emit_wd_part2(hc):
        qb = hc % 2
        land = wd_land[qb]
        p.wait("scalar", "VE", wd_state["ve127_b"][qb])
        for half in range(2):
            se = p.inc("scalar", lambda e, half=half, land=land, qb=qb: e.activation(
                land[half][:, :], land[half][:, :], AF.Identity,
                bias=cmagb[:, :], scale=wdinvq[qb][:, :]), "SE")
            p.wait("vector", "SE", se)
            if half == 0:
                p.wait("vector", "PE", wd_state["pe_b"][qb])
            for q in range(2):
                p.emit("vector", lambda e, half=half, q=q, land=land: e.tensor_single_scalar(
                    wdi8[:, :], land[half][:, q * HQ4:(q + 1) * HQ4], -C23, ALU.add))
                wd_state["ge_b"][qb] = p.inc("vector", lambda e, half=half, q=q, qb=qb: e.tensor_scalar_mul(
                    wd_q[qb][:, half * HI + q * HQ4:half * HI + (q + 1) * HQ4],
                    wdi8[:, :], sdh2[qb][:, :]), "VE")
            wd_state["halfq"][half] = wd_state["ge_b"][qb]
        wd_state["act_se"][qb] = p.cnt["SE"]

    def emit_wd_transpose(hc):
        for half in range(2):
            p.wait("tensor", "VE", wd_state["halfq"][half])
            nloc = NIC // 2
            p.wait("tensor", "SE")
            p.wait("tensor", "dS", wd_state["sdma"])
            for gstart in range(0, nloc, 8):
                n = min(8, nloc - gstart)
                p.wait("tensor", "SE")
                for j in range(n):
                    lic = gstart + j
                    fn = lambda e, lic=lic, j=j, half=half, hc=hc: e.transpose(
                        ps_tr[:, j * 128:(j + 1) * 128],
                        wd_q[hc % 2][:, half * HI + lic * 128:half * HI + (lic + 1) * 128],
                        eye[:, :])
                    if j == n - 1:
                        wd_state["pe"] = p.inc("tensor", fn, "PE")
                        wd_state["pe_b"][hc % 2] = wd_state["pe"]
                    else:
                        p.emit("tensor", fn)
                p.wait("scalar", "PE", wd_state["pe"])
                p.inc("scalar", lambda e, gstart=gstart, n=n: e.copy(
                    wd_stage[:, gstart * 128:(gstart + n) * 128],
                    ps_tr[:, 0:n * 128]), "SE")
            se = p.cnt["SE"]
            p.wait("sync", "SE", se)
            g0 = half * nloc
            wd_state["sdma"] = dma(
                "dS", "sync",
                wdTsc3[:, g0:g0 + nloc, hc * 128:(hc + 1) * 128],
                wd_stage[:, :].rearrange("p (c t) -> p c t", c=nloc))

    def wd_hook(ic):
        if ic % 3 == 0 and ic // 3 < NHC - 1:
            emit_wd_part1(ic // 3)
        if ic == 19:
            emit_wd_part1(7)
        if ic % 3 == 0 and ic >= 3:
            emit_wd_transpose((ic - 3) // 3)

    def wd_post(ic):
        if ic % 3 == 0 and ic // 3 < NHC - 1:
            emit_wd_part2(ic // 3)
        if ic == 19:
            emit_wd_part2(7)
        if ic == NIC - 1:
            emit_wd_transpose(7)

    mm1_end = emit_gateup(0, se_actT, bufs1, 0, wd_hook=wd_hook, wd_post=wd_post)
    hstctx.close()
    wdct.close()
    s1ctx.close()
    wdT_ctx = ExitStack()
    wdT = sb("wdT", [128, NIC * D], BF16, wdT_ctx)
    dwn1 = ExitStack()
    bufs_d1 = {
        "hq_bf": [sb(f"hqb{i}", [128, NIC * 128], BF16, dwn1) for i in range(2)],
        "hqr2": [sb(f"hqr2f{i}", [128, NIC * 128], F32, dwn1) for i in range(2)],
        "hqi8": [sb("hqi8f0", [128, NIC * 128], INT8, dwn1)] * 2,
    }
    p.wait("gpsimd", "PE", wd_state["pe"])
    p.wait("gpsimd", "VE")
    p.wait("gpsimd", "dS", wd_state["sdma"])
    HW1 = (NIC // 2) * D
    d_wdTa = dma("dG", "gpsimd", wdT[:, 0:HW1], wdTsc[:, 0:HW1])
    p.wait("gpsimd", "dS")
    g0 = down_gather(bufs_d1["hqr2"][0], 0, TW[0])
    d_wdTb = dma("dG", "gpsimd", wdT[:, HW1:NIC * D], wdTsc[:, HW1:NIC * D])
    cc2 = emit_hbarrier_start(1)
    sc2 = emit_hbarrier_fin(1, 2, cc2)
    p.wait("tensor", "dG", d_wdTa)
    emit_down(2, sc2, wdT, bufs_d1, bn_hook=emit_bn, pre_g=g0, wdTb=d_wdTb)

    phase_waits()
    dwn1.close()
    wdT_ctx.close()

    if DEBUG_STAGE == 1:
        return _finish(nc, p, dma, x_res, out_h, dump=(sca, bc, rowb, hmaxc, wgTsc, wuTsc, hsc, sg1, su1, sga))

    att1 = ExitStack()
    W1sb = sb("W1sb", [128, NHC * E1], BF16, att1)
    b2r = sb("b2r", [1, D], BF16, att1)
    dma("dG", "gpsimd", b2r[0:1, :], b2_h[None, :])
    diag2 = sb("diag2", [128, KCONV * 128], BF16, att1)
    att2 = ExitStack()
    GPW = TR + 2 * HALO
    glup = sb("glup", [128, NHC * GPW], BF16, att2)
    sig_t = silu_t

    def diag_ap(db, k):
        if db == 0:
            if k < 8:
                return qtm[0][:, k * 128:(k + 1) * 128]
            return qtm[1][:, (k - 8) * 128:(k - 7) * 128]
        return diag2[:, k * 128:(k + 1) * 128]

    for dc in range(NHC):
        dma("dG", "gpsimd", W1sb[:, dc * E1:(dc + 1) * E1], w1_h[dc * 128:(dc + 1) * 128, :])

    emit_ln_tail()
    pe_of = [0, 0]
    for k in range(NT):
        w = TW[k]
        b = k % 2
        p.wait("scalar", "PE", pe_of[b])
        p.wait("scalar", "VE")
        se = p.inc("scalar", lambda e, k=k, w=w, b=b: e.activation(
            qtm[b][0:w, :], x_tile(k), AF.Identity,
            bias=nmr[0:w, k:k + 1], scale=rstd[0:w, k:k + 1]), "SE")
        p.wait("tensor", "SE", se)
        p.wait("tensor", "SE")
        for hc in range(NHC):
            fn = lambda e, hc=hc, w=w, b=b: e.transpose(
                ps_tr[:, hc * 128:hc * 128 + w], qtm[b][0:w, hc * 128:(hc + 1) * 128], eye[0:w, 0:w])
            if hc == NHC - 1:
                pe = p.inc("tensor", fn, "PE")
            else:
                p.emit("tensor", fn)
        pe_of[b] = pe
        p.wait("scalar", "PE", pe)
        p.inc("scalar", lambda e, k=k, w=w: e.copy(
            actT3[:, :, k * 128:k * 128 + w], ps_tr_view(w)), "SE")
    se_y2T = p.cnt["SE"]

    for dc in range(NHC):
        p.emit("vector", lambda e, dc=dc: e.memset(glup[:, dc * GPW:dc * GPW + HALO], 0.0))
        p.emit("vector", lambda e, dc=dc: e.memset(
            glup[:, dc * GPW + HALO + TR:(dc + 1) * GPW], 0.0))
    ve_gmem = p.inc("vector", lambda e: e.engine_nop(), "VE")

    p.wait("tensor", "SE", se_y2T)
    p.wait("tensor", "dG")
    glu_ve_b = [0, 0]
    sig_se = [0, 0]
    for ec in range(NHC):
        for it, (t0, tw_) in enumerate(TCH):
            pb = it % 2
            p.wait("tensor", "VE", glu_ve_b[pb])
            p.wait("tensor", "SE", sig_se[pb])
            for half, pst in ((0, ps_a), (1, ps_b)):
                ecc = ec + half * NHC
                for dc in range(NHC):
                    fn = lambda e, dc=dc, ecc=ecc, t0=t0, tw_=tw_, pst=pst, pb=pb: e.matmul(
                        pst[pb][:, 0:tw_],
                        W1sb[:, dc * E1 + ecc * 128:dc * E1 + (ecc + 1) * 128],
                        actT[:, dc * TR + t0:dc * TR + t0 + tw_],
                        start=(dc == 0), stop=(dc == NHC - 1))
                    if half == 1 and dc == NHC - 1:
                        pe = p.inc("tensor", fn, "PE")
                    else:
                        p.emit("tensor", fn)
            p.wait("scalar", "PE", pe)
            p.wait("scalar", "VE")
            sig_se[pb] = p.inc("scalar", lambda e, ec=ec, tw_=tw_, pb=pb: e.activation(
                sig_t[pb][:, 0:tw_], ps_b[pb][:, 0:tw_], AF.Sigmoid,
                bias=b1c[:, NHC + ec:NHC + ec + 1]), "SE")
            p.wait("vector", "SE", sig_se[pb])
            p.wait("vector", "VE", ve_gmem)
            glu_ve_b[pb] = p.inc("vector", lambda e, ec=ec, t0=t0, tw_=tw_, pb=pb: e.scalar_tensor_tensor(
                glup[:, ec * GPW + HALO + t0:ec * GPW + HALO + t0 + tw_],
                ps_a[pb][:, 0:tw_], b1c[:, ec:ec + 1], sig_t[pb][:, 0:tw_],
                ALU.add, ALU.mult), "VE")
    p.wait("vector", "VE", max(glu_ve_b))
    for dc in range(NHC):
        p.emit("vector", lambda e, dc=dc: e.tensor_scalar_mul(
            glup[:, dc * GPW + HALO:dc * GPW + 2 * HALO],
            glup[:, dc * GPW + HALO:dc * GPW + 2 * HALO], edge[:, 0:1]))
        p.emit("vector", lambda e, dc=dc: e.tensor_scalar_mul(
            glup[:, dc * GPW + HALO + TR - HALO:dc * GPW + HALO + TR],
            glup[:, dc * GPW + HALO + TR - HALO:dc * GPW + HALO + TR], edge[:, 1:2]))
    ve_glu = p.inc("vector", lambda e: e.engine_nop(), "VE")

    convout = actT
    convout3 = actT3
    conv_evac = [0, 0]
    pe_conv_b = [0, 0]
    diag_ve = [0, 0]

    def build_diag(dc):
        db = dc % 2
        p.wait("vector", "PE", pe_conv_b[db])
        for k in range(KCONV):
            fn = lambda e, dc=dc, k=k, db=db: e.tensor_scalar_mul(
                diag_ap(db, k), eye[:, :],
                filt[:, dc * KCONV + k:dc * KCONV + k + 1])
            p.emit("vector", fn)
        diag_ve[db] = p.inc("vector", lambda e: e.engine_nop(), "VE")

    p.wait("vector", "VE", ve_glu)
    build_diag(0)
    for dc in range(NHC):
        db = dc % 2
        if dc + 1 < NHC:
            build_diag(dc + 1)
        p.wait("tensor", "VE", diag_ve[db])
        for it, (t0, tw_) in enumerate(TCH):
            pb = it % 2
            p.wait("tensor", "SE", conv_evac[pb])
            for k in range(KCONV):
                fn = lambda e, dc=dc, k=k, t0=t0, tw_=tw_, pb=pb, db=db: e.matmul(
                    ps_c[pb][:, 0:tw_], diag_ap(db, k),
                    glup[:, dc * GPW + t0 + k:dc * GPW + t0 + k + tw_],
                    start=(k == 0), stop=(k == KCONV - 1))
                if k == KCONV - 1:
                    pe_conv_b[db] = p.inc("tensor", fn, "PE")
                else:
                    p.emit("tensor", fn)
            p.wait("scalar", "PE", pe_conv_b[db])
            conv_evac[pb] = p.inc("scalar", lambda e, dc=dc, t0=t0, tw_=tw_, pb=pb: e.copy(
                convout[:, dc * TR + t0:dc * TR + t0 + tw_], ps_c[pb][:, 0:tw_]), "SE")

    se_conv = p.cnt["SE"]
    p.wait("gpsimd", "PE")
    p.wait("gpsimd", "SE", se_conv)
    att2.close()
    att3 = ExitStack()
    W2sb = sb("W2sb", [128, NHC * D], BF16, att3)
    dW2 = 0
    for dc in range(NHC):
        dW2 = dma("dG", "gpsimd", W2sb[:, dc * D:(dc + 1) * D], w2_h[dc * 128:(dc + 1) * 128, :])
    p.wait("tensor", "SE", se_conv)
    p.wait("tensor", "dG", dW2)
    w2_ve = [0, 0]
    for k in range(NT):
        w = TW[k]
        for e2 in range(2):
            p.wait("tensor", "VE", w2_ve[e2])
            for dc in range(NHC):
                fn = lambda e, dc=dc, e2=e2, k=k, w=w: e.matmul(
                    ps_c[e2][0:w, 0:512],
                    convout[:, dc * TR + k * 128:dc * TR + k * 128 + w],
                    W2sb[:, dc * D + e2 * 512:dc * D + e2 * 512 + 512],
                    start=(dc == 0), stop=False)
                p.emit("tensor", fn)
            fn = lambda e, e2=e2, k=k, w=w: e.matmul(
                ps_c[e2][0:w, 0:512], onesb[0:1, 0:w], b2r[0:1, e2 * 512:e2 * 512 + 512],
                start=False, stop=True)
            pe = p.inc("tensor", fn, "PE")
            p.wait("vector", "PE", pe)
            w2_ve[e2] = p.inc("vector", lambda e, e2=e2, k=k, w=w: e.tensor_tensor(
                x_res[0:w, k * D + e2 * 512:k * D + e2 * 512 + 512],
                ps_c[e2][0:w, 0:512],
                x_res[0:w, k * D + e2 * 512:k * D + e2 * 512 + 512], ALU.add), "VE")
        if k == 0:
            p.emit("vector", lambda e: e.memset(x_res[0:HALO, 0:D], 0.0))
        if k == NT - 1:
            p.wait("sync", "VE", w2_ve[1])
            dz = dma("dS", "sync", x_res[RAG - HALO:RAG, (NT - 1) * D:NT * D],
                     x_h[TPAD - HALO:TPAD, :])
            p.wait("vector", "dS", dz)
        emit_bn(k, w)
        p.emit("vector", lambda e, k=k, w=w: e.tensor_reduce(
            xmx[0:w, k:k + 1], x_res[0:w, k * D:(k + 1) * D], AX.X, ALU.max))
        p.emit("vector", lambda e, k=k, w=w: e.tensor_reduce(
            xmn[0:w, k:k + 1], x_res[0:w, k * D:(k + 1) * D], AX.X, ALU.min))

    phase_waits()
    att3.close()
    att1.close()

    if DEBUG_STAGE == 2:
        return _finish(nc, p, dma, x_res, out_h, dump=(sca, bc, rowb, hmaxc, wgTsc, wuTsc, hsc, sg1, su1, sga))

    emit_ln_tail()
    mean_v = mv[:, :].rearrange("p (k two) -> p k two", two=2)[:, :, 0]
    p.emit("vector", lambda e: e.tensor_tensor(
        xmx[:, :], xmx[:, :], mean_v, ALU.subtract))
    p.emit("vector", lambda e: e.tensor_tensor(
        xmn[:, :], mean_v, xmn[:, :], ALU.subtract))
    vsp()
    p.emit("vector", lambda e: e.tensor_tensor(xmx[:, :], xmx[:, :], xmn[:, :], ALU.max))
    p.emit("vector", lambda e: e.tensor_tensor(xmx[:, :], xmx[:, :], rstd[:, :], ALU.mult))
    vsp()
    p.wait("vector", "dS")
    ve3 = p.inc("vector", lambda e: e.tensor_reduce(
        rmx[:, :], xmx[:, :], AX.X, ALU.max), "VE")
    p.wait("sync", "VE", ve3)
    gve3 = cross_core_max(2, rmx[:, 0:1])
    sc3 = make_scales(2, 0, 0, gve3)
    p.wait("vector", "VE", sc3)
    se_actT3v = emit_quant_lnq(0)

    wdT_ctx2 = ExitStack()
    wdT2 = sb("wdT2", [128, NIC * D], BF16, wdT_ctx2)
    d_wdT = dma("dG", "gpsimd", wdT2[:, :], wdTsc[:, :])
    s2ctx = ExitStack()
    bufs2 = {
        "wgT": [sb(f"wgT2_{i}", [128, D], BF16, s2ctx) for i in range(2)],
        "h_stage": [sb(f"hst2_{i}", [128, TR], F16, s2ctx) for i in range(2)],
        "wuT": [sb(f"wuT2_{i}", [128, D], BF16, s2ctx) for i in range(2)],
    }

    mm2_end = emit_gateup(1, se_actT3v, bufs2, 0)
    cc4 = emit_hbarrier_start(3)
    sc4 = emit_hbarrier_fin(3, 2, cc4)
    p.wait("sync", "PE", mm2_end)
    s2ctx.close()

    dwn2 = ExitStack()
    bufs_d2 = {
        "hq_bf": [sb(f"hqb2_{i}", [128, NIC * 128], BF16, dwn2) for i in range(2)],
        "hqr2": [sb(f"hqr2g{i}", [128, NIC * 128], F32, dwn2) for i in range(2)],
        "hqi8": [sb("hqi8g0", [128, NIC * 128], INT8, dwn2)] * 2,
    }
    p.wait("tensor", "dG", d_wdT)
    ve_done = emit_down(2, sc4, wdT2, bufs_d2)

    p.wait("sync", "VE", ve_done)
    nc2 = _finish(nc, p, dma, x_res, out_h, dump=(sca, bc, rowb, hmaxc, wgTsc, wuTsc, hsc, sg1, su1, sga))
    dwn2.close()
    wdT_ctx2.close()
    ctx.close()
    return nc2


_NC_CACHE = None


def _get_nc():
    global _NC_CACHE
    if _NC_CACHE is None:
        _NC_CACHE = build_nc()
    return _NC_CACHE


def _stage_inputs(x, W1, b1, W2, b2, csw, clip_x, clip_down, wg, wu, wd):
    x = np.asarray(x, dtype=np.float32)
    eye = np.eye(128, dtype=ml_dtypes.bfloat16)
    onesf = np.ones((1, 128), dtype=np.float32)
    onesb = np.ones((1, 128), dtype=ml_dtypes.bfloat16)
    sel = np.zeros((16, NHC * 128), dtype=ml_dtypes.bfloat16)
    for dc in range(NHC):
        for pp in range(128):
            sel[2 * dc + pp // 64, dc * 128 + pp] = 1.0
    clips = np.array([[float(np.asarray(clip_x).ravel()[0]),
                       float(np.asarray(clip_down).ravel()[0])]], dtype=np.float32)
    common = {
        "w1": np.ascontiguousarray(np.asarray(W1, dtype=np.float32)),
        "b1": np.ascontiguousarray(np.asarray(b1, dtype=np.float32)),
        "w2": np.ascontiguousarray(np.asarray(W2, dtype=np.float32)),
        "b2": np.ascontiguousarray(np.asarray(b2, dtype=np.float32)),
        "csw": np.ascontiguousarray(np.asarray(csw, dtype=np.float32)),
        "clips": clips,
        "wg": np.ascontiguousarray(np.asarray(wg, dtype=np.float32)),
        "wu": np.ascontiguousarray(np.asarray(wu, dtype=np.float32)),
        "wd": np.ascontiguousarray(np.asarray(wd, dtype=np.float32)),
        "eyeI": eye,
        "onesf": onesf,
        "onesb": onesb,
        "sel": sel,
    }
    in_maps = []
    for c in range(8):
        b, s = c // 2, c % 2
        start = s * TOK - HALO
        xs = np.zeros((TPAD, D), dtype=np.float32)
        lo = max(start, 0)
        hi = min(start + TR, S)
        xs[lo - start:hi - start] = x[b, lo:hi]
        edgev = np.ones((128, 2), dtype=np.float32)
        edgev[:, 0 if s == 0 else 1] = 0.0
        m = dict(common)
        m["x"] = xs
        m["edge"] = edgev
        in_maps.append(m)
    return in_maps


def kernel(L0_attn_inputs, L0_attn_W1, L0_attn_b1, L0_attn_W2, L0_attn_b2,
           L0_attn_conv_shared_weight, L0_ffn_clip_x, L0_ffn_clip_down,
           L0_ffn_w_gate, L0_ffn_w_up, L0_ffn_w_down):
    nc = _get_nc()
    in_maps = _stage_inputs(L0_attn_inputs, L0_attn_W1, L0_attn_b1, L0_attn_W2,
                            L0_attn_b2, L0_attn_conv_shared_weight,
                            L0_ffn_clip_x, L0_ffn_clip_down,
                            L0_ffn_w_gate, L0_ffn_w_up, L0_ffn_w_down)
    res = run_bass_kernel_spmd(nc, in_maps, core_ids=list(range(8)))
    out = np.empty((B, S, D), dtype=np.float32)
    for c in range(8):
        b, s = c // 2, c % 2
        out[b, s * TOK:(s + 1) * TOK] = res.results[c]["out"]
    return out


# version 26 (final)
# speedup vs baseline: 1.0269x; 1.0269x over previous
import os
import numpy as np
import ml_dtypes
from contextlib import ExitStack

DEBUG_STAGE = int(os.environ.get("KMK_DEBUG_STAGE", "0"))
DEBUG_DUMP = int(os.environ.get("KMK_DEBUG_DUMP", "0"))

import concourse.bass as bass
import concourse.mybir as mybir
from concourse.bass_utils import run_bass_kernel_spmd

F32 = mybir.dt.float32
BF16 = mybir.dt.bfloat16
F16 = mybir.dt.float16
INT8 = mybir.dt.int8
AF = mybir.ActivationFunctionType
ALU = mybir.AluOpType
AX = mybir.AxisListType

D = 1024
E1 = 2048
I = 2816
B, S = 4, 4096
TOK = 2048
HALO = 7
TR = TOK + 2 * HALO
NT = 17
TPAD = NT * 128
RAG = TR - 16 * 128
NHC = D // 128
NIC = I // 128
KCONV = 15
QMAX = 127.0
C23 = 12582912.0
CLO = C23 - 128.0
CHI = C23 + 127.0
EPS = 1e-5
TCH = [(0, 512), (512, 512), (1024, 512), (1536, 512), (2048, RAG)]
TW = [128] * 16 + [RAG]

ENGS = ("sync", "gpsimd", "tensor", "vector", "scalar")


class Prog:
    def __init__(self, nc):
        self.nc = nc
        self.q = {e: [] for e in ENGS}
        self.cnt = {}
        self.sems = {}

    def reg_sem(self, name, handle):
        self.sems[name] = handle
        self.cnt[name] = 0

    def emit(self, eng, fn):
        self.q[eng].append(fn)

    def inc(self, eng, fn, sem, n=1):
        self.cnt[sem] += n
        self.q[eng].append(
            lambda e, fn=fn, s=sem, n=n: fn(e).then_inc(self.sems[s], n))
        return self.cnt[sem]

    def wait(self, eng, sem, v=None):
        if v is None:
            v = self.cnt[sem]
        assert v <= self.cnt[sem], (sem, v, self.cnt[sem])
        if v > 0:
            self.q[eng].append(lambda e, s=sem, v=v: e.wait_ge(self.sems[s], v))

    def run(self, eng_obj, eng_name):
        for fn in self.q[eng_name]:
            fn(eng_obj)


def _finish(nc, p, dma, x_res, out_h, dump=None, stores_done=False):
    for eng in ("sync",):
        p.wait(eng, "VE")
        p.wait(eng, "SE")
        p.wait(eng, "PE")
        p.wait(eng, "GE")
    d_out = 0
    for k in ([] if stores_done else range(NT)):
        if k == 0:
            src = x_res[HALO:128, 0:D]
            dst = out_h[0:121, :]
        elif k == NT - 1:
            src = x_res[0:RAG - HALO, k * D:(k + 1) * D]
            dst = out_h[128 * k - HALO:TOK, :]
        else:
            src = x_res[0:128, k * D:(k + 1) * D]
            dst = out_h[128 * k - HALO:128 * (k + 1) - HALO, :]
        d_out = dma("dS", "sync", dst, src)
    if DEBUG_DUMP and dump is not None:
        sca, bc_, rowb, hmaxc, wgTsc_, wuTsc_, hsc_, sg1_, su1_, sga_ = dump
        d_out = dma("dS", "sync", out_h[100:101, 0:8], sca[0:1, :])
        d_out = dma("dS", "sync", out_h[101:102, 0:4], bc_[0:1, 0:4])
        d_out = dma("dS", "sync", out_h[102:103, 0:128], rowb[0:1, :])
        d_out = dma("dS", "sync", out_h[103:231, 0:110], hmaxc[:, :])
        p.wait("gpsimd", "dS", d_out)
        dg = dma("dG", "gpsimd", out_h[231:359, 0:1024], wgTsc_[0])
        dg = dma("dG", "gpsimd", out_h[359:487, 0:1024], wuTsc_[0])
        dg = dma("dG", "gpsimd", out_h[487:615, 0:1024], hsc_[0][:, 0:1024])
        dg = dma("dG", "gpsimd", out_h[615:743, 0:22], sg1_[:, :])
        dg = dma("dG", "gpsimd", out_h[743:871, 0:22], su1_[:, :])
        dg = dma("dG", "gpsimd", out_h[871:999, 0:4], bc_[:, :])
        dg = dma("dG", "gpsimd", out_h[999:1127, 0:22], sga_[:, :])
        p.wait("sync", "dG", dg)
    p.wait("sync", "dS")

    with nc.Block() as block:
        @block.sync
        def _(eng):
            p.run(eng, "sync")

        @block.gpsimd
        def _(eng):
            p.run(eng, "gpsimd")

        @block.tensor
        def _(eng):
            p.run(eng, "tensor")

        @block.vector
        def _(eng):
            p.run(eng, "vector")

        @block.scalar
        def _(eng):
            p.run(eng, "scalar")

    return nc


def build_nc():
    nc = bass.Bass("TRN2", target_bir_lowering=False,
                   detect_race_conditions=False)

    x_h = nc.dram_tensor("x", [TPAD, D], F32, kind="ExternalInput").ap()
    w1_h = nc.dram_tensor("w1", [D, E1], F32, kind="ExternalInput").ap()
    b1_h = nc.dram_tensor("b1", [E1], F32, kind="ExternalInput").ap()
    w2_h = nc.dram_tensor("w2", [D, D], F32, kind="ExternalInput").ap()
    b2_h = nc.dram_tensor("b2", [D], F32, kind="ExternalInput").ap()
    csw_h = nc.dram_tensor("csw", [16, KCONV], F32, kind="ExternalInput").ap()
    clip_h = nc.dram_tensor("clips", [1, 2], F32, kind="ExternalInput").ap()
    wg_h = nc.dram_tensor("wg", [I, D], F32, kind="ExternalInput").ap()
    wu_h = nc.dram_tensor("wu", [I, D], F32, kind="ExternalInput").ap()
    wd_h = nc.dram_tensor("wd", [D, I], F32, kind="ExternalInput").ap()
    eye_h = nc.dram_tensor("eyeI", [128, 128], BF16, kind="ExternalInput").ap()
    onesf_h = nc.dram_tensor("onesf", [1, 128], F32, kind="ExternalInput").ap()
    onesb_h = nc.dram_tensor("onesb", [1, 128], BF16, kind="ExternalInput").ap()
    sel_h = nc.dram_tensor("sel", [16, NHC * 128], BF16, kind="ExternalInput").ap()
    edge_h = nc.dram_tensor("edge", [128, 2], F32, kind="ExternalInput").ap()
    out_h = nc.dram_tensor("out", [TOK, D], F32, kind="ExternalOutput").ap()

    hsc = nc.dram_tensor("hsc", [NIC, 128, TR], F16, kind="Internal").ap()
    wgTsc = nc.dram_tensor("wgTsc", [NIC, 128, D], BF16, kind="Internal").ap()
    wuTsc = nc.dram_tensor("wuTsc", [NIC, 128, D], BF16, kind="Internal").ap()
    wdTsc = nc.dram_tensor("wdTsc", [128, NIC * D], BF16, kind="Internal").ap()
    ccin = [nc.dram_tensor(f"ccin{i}", [128], F32, kind="Internal").ap() for i in range(5)]
    bb = nc.dram_tensor("bb", [2], F32, kind="Internal").ap()
    ccout = [nc.dram_tensor(f"ccout{i}", [128], F32, kind="Internal").ap() for i in range(5)]

    p = Prog(nc)
    ctx = ExitStack()
    proctx = ExitStack()

    def sb(name, shape, dtype, st=None):
        return (st or ctx).enter_context(nc.sbuf_tensor(name, shape, dtype))

    for s in ("dS", "dG", "PE", "VE", "SE", "GE", "CC"):
        p.reg_sem(s, ctx.enter_context(nc.semaphore(s)))

    x_res = sb("x_res", [128, NT * D], F32)
    actT = sb("actT", [128, NHC * TR], BF16)
    eye = sb("eye_s", [128, 128], BF16)
    onesb = sb("onesb_s", [1, 128], BF16)
    edge = sb("edge_s", [128, 2], F32)
    b1c = sb("b1c", [128, 2 * NHC], F32)
    filt = sb("filt", [128, NHC * KCONV], F32)
    cswb = sb("cswb", [16, KCONV], F32)
    clip2 = sb("clip2", [1, 2], F32)
    sig2 = sb("sig2", [1, 2], F32)
    csmax = sb("csmax", [16, 1], F32)
    cssum = sb("cssum", [16, 1], F32)
    softw = sb("softw", [16, KCONV], F32)
    stat6 = sb("stat6", [128, 12], F32)
    mv = sb("mv", [128, 2 * NT], F32)
    rstd = sb("rstd", [128, NT], F32)
    nmr = sb("nmr", [128, NT], F32)
    rs1 = sb("rs1", [128, NT], F32)
    bia1 = sb("bia1", [128, NT], F32)
    ymaxc = sb("ymaxc", [128, NT], F32)
    hmaxc = sb("hmaxc", [128, NIC], F32)
    sga = sb("sga", [128, NIC], F32)
    sua = sb("sua", [128, NIC], F32)
    sg1 = sb("sg1", [128, NIC], F32)
    su1 = sb("su1", [128, NIC], F32)
    rowb = sb("rowb", [1, 128], F32)
    sca = sb("sca", [1, 8], F32)
    bc = sb("bc", [128, 4], F32)
    invq = sb("invq", [128, 1], F32)
    sdh = sb("sdh", [128, 1], F32)
    rmx = sb("rmx", [128, 1], F32)
    spc = sb("spc", [128, 1], F32)
    epsb = sb("epsb", [128, 1], F32)
    cmagb = sb("cmagb", [128, 1], F32)
    ncmagb = sb("ncmagb", [128, 1], F32)
    xmx = sb("xmx", [128, NT], F32)
    xmn = sb("xmn", [128, NT], F32)
    r2t = [sb(f"r2t{i}", [128, D], F32) for i in range(2)]
    qtm = [sb(f"qtm{i}", [128, D], BF16) for i in range(2)]
    qint = [sb("qint0", [128, D], INT8)] * 2
    silu_t = [sb(f"sil{i}", [128, 512], F32) for i in range(2)]

    ps_tr = ctx.enter_context(nc.psum_tensor("ps_tr", [128, 1024], BF16))
    ps_a = [ctx.enter_context(nc.psum_tensor(f"ps_a{i}", [128, 512], F32)) for i in range(2)]
    ps_b = [ctx.enter_context(nc.psum_tensor(f"ps_b{i}", [128, 512], F32)) for i in range(2)]
    ps_c = [ctx.enter_context(nc.psum_tensor(f"ps_c{i}", [128, 512], F32)) for i in range(2)]
    ps_m = ctx.enter_context(nc.psum_tensor("ps_m", [128, 512], F32))

    actT3 = actT[:, :].rearrange("p (c t) -> p c t", c=NHC)

    def ps_tr_view(w):
        return ps_tr[:, :].rearrange("p (c x) -> p c x", c=NHC)[:, :, 0:w]

    def x_tile(k):
        return x_res[0:TW[k], k * D:(k + 1) * D]

    def dma(sem, eng, out, in_):
        return p.inc(eng, lambda e, o=out, i=in_: e.dma_start(out=o, in_=i), sem, 16)

    def phase_barrier():
        marks = {}
        marks["PE"] = p.inc("tensor", lambda e: e.matmul(
            ps_m[0:1, 0:1], eye[0:1, 0:1], eye[0:1, 0:1], start=True, stop=True), "PE")
        marks["VE"] = p.inc("vector", lambda e: e.engine_nop(), "VE")
        marks["SE"] = p.inc("scalar", lambda e: e.copy(sca[0:1, 7:8], epsb[0:1, 0:1]), "SE")
        marks["GE"] = p.inc("gpsimd", lambda e: e.engine_nop(), "GE")
        for eng in ENGS:
            for s in ("PE", "VE", "SE", "GE"):
                if (eng, s) in (("tensor", "PE"), ("vector", "VE"),
                                ("scalar", "SE"), ("gpsimd", "GE")):
                    continue
                p.wait(eng, s, marks[s])
        for eng in ("tensor", "vector", "scalar", "gpsimd"):
            p.wait(eng, "dS")
            p.wait(eng, "dG")

    def phase_waits():
        for eng in ("sync", "gpsimd"):
            for s in ("PE", "VE", "SE"):
                p.wait(eng, s)

    def vsp(n=2):
        for _ in range(n):
            p.emit("vector", lambda e: e.tensor_copy(spc[:, :], epsb[:, :]))

    xh3 = x_h.rearrange("(n p) d -> p n d", p=128)
    xr3 = x_res[:, :].rearrange("p (n d) -> p n d", n=NT)
    dxh = dma("dS", "sync", xr3[:, 0:9, :], xh3[:, 0:9, :])
    dx = dma("dG", "gpsimd", xr3[:, 9:NT, :], xh3[:, 9:NT, :])
    dma("dS", "sync", eye[:, :], eye_h)
    dma("dS", "sync", onesb[0:1, :], onesb_h)
    dma("dS", "sync", edge[:, :], edge_h)
    p.inc("sync", lambda e: e.dma_start(
        out=b1c[:, :], in_=b1_h.rearrange("(n p) -> p n", p=128),
        allow_slow_non_contiguous=True), "dS", 16)
    selb = sb("selb", [16, NHC * 128], BF16, proctx)
    softwb = sb("softwb", [16, KCONV], BF16, proctx)
    dma("dS", "sync", selb[:, :], sel_h)
    dma("dS", "sync", cswb[:, :], csw_h)
    dcl = dma("dS", "sync", clip2[0:1, :], clip_h)
    dwu = dma("dS", "sync", ccin[4].rearrange("(p one) -> p one", one=1),
              edge[:, 0:1])
    p.wait("gpsimd", "dS", dwu)
    p.inc("gpsimd", lambda e: e.collective_compute(
        "AllReduce", ALU.max, replica_groups=[list(range(8))],
        ins=[ccin[4].opt()], outs=[ccout[4].opt()]), "CC")

    p.emit("vector", lambda e: e.memset(mv[:, 2 * (NT - 1):2 * NT], 0.0))
    p.emit("vector", lambda e: e.memset(ymaxc[:, NT - 1:NT], 0.0))
    p.emit("vector", lambda e: e.memset(epsb[:, :], EPS))
    p.emit("vector", lambda e: e.memset(ncmagb[:, :], -C23))
    p.emit("vector", lambda e: e.memset(xmx[:, :], 0.0))
    p.emit("vector", lambda e: e.memset(xmn[:, :], 0.0))
    p.inc("vector", lambda e: e.memset(cmagb[:, :], C23), "VE")
    p.wait("scalar", "VE")
    p.wait("scalar", "dS", dcl)
    p.inc("scalar", lambda e: e.activation(sig2[0:1, :], clip2[0:1, :], AF.Sigmoid), "SE")

    p.wait("vector", "dS", dcl)
    p.emit("vector", lambda e: e.tensor_reduce(csmax[:, :], cswb[:, :], AX.X, ALU.max))
    p.emit("vector", lambda e: e.tensor_copy(sca[0:1, 6:7], epsb[0:1, 0:1]))
    p.emit("vector", lambda e: e.tensor_copy(sca[0:1, 7:8], epsb[0:1, 0:1]))
    ve = p.inc("vector", lambda e: e.tensor_scalar_mul(csmax[:, :], csmax[:, :], -1.0), "VE")
    p.wait("scalar", "VE", ve)
    se = p.inc("scalar", lambda e: e.activation(softw[:, :], cswb[:, :], AF.Exp, bias=csmax[:, :]), "SE")
    p.wait("vector", "SE", se)
    p.emit("vector", lambda e: e.tensor_reduce(cssum[:, :], softw[:, :], AX.X, ALU.add))
    p.emit("vector", lambda e: e.tensor_copy(sca[0:1, 6:7], epsb[0:1, 0:1]))
    p.emit("vector", lambda e: e.tensor_copy(sca[0:1, 7:8], epsb[0:1, 0:1]))
    p.emit("vector", lambda e: e.reciprocal(cssum[:, :], cssum[:, :]))
    p.emit("vector", lambda e: e.tensor_copy(sca[0:1, 6:7], epsb[0:1, 0:1]))
    p.emit("vector", lambda e: e.tensor_copy(sca[0:1, 7:8], epsb[0:1, 0:1]))
    p.emit("vector", lambda e: e.tensor_scalar_mul(softw[:, :], softw[:, :], cssum[:, :]))
    vsp()
    ve_sw = p.inc("vector", lambda e: e.tensor_copy(softwb[:, :], softw[:, :]), "VE")
    p.wait("tensor", "VE", ve_sw)
    p.wait("tensor", "dS")
    for dc in range(NHC):
        p.wait("tensor", "VE")
        pe = p.inc("tensor", lambda e, dc=dc: e.matmul(
            ps_m[:, 0:KCONV], selb[:, dc * 128:(dc + 1) * 128], softwb[:, :],
            start=True, stop=True), "PE")
        p.wait("vector", "PE", pe)
        p.inc("vector", lambda e, dc=dc: e.tensor_copy(
            filt[:, dc * KCONV:(dc + 1) * KCONV], ps_m[:, 0:KCONV]), "VE")

    proctx.close()

    def emit_bn(k, w):
        p.emit("vector", lambda e, k=k, w=w: e.bn_stats(
            stat6[0:w, 0:6], x_res[0:w, k * D:k * D + 512]))
        p.emit("vector", lambda e, k=k, w=w: e.bn_stats(
            stat6[0:w, 6:12], x_res[0:w, k * D + 512:(k + 1) * D]))
        vsp()
        p.emit("vector", lambda e, k=k, w=w: e.bn_aggr(
            mv[0:w, 2 * k:2 * k + 2], stat6[0:w, :]))

    def emit_ln():
        p.wait("vector", "dS", dxh)
        p.wait("vector", "SE")
        for k in range(NT):
            if k == 9:
                p.wait("vector", "dG", dx)
            emit_bn(k, TW[k])
        return emit_ln_tail()

    def emit_ln_tail():
        vlast = p.inc("vector", lambda e: e.engine_nop(), "VE")
        p.wait("scalar", "VE", vlast)
        var_view = mv[:, :].rearrange("p (k two) -> p k two", two=2)[:, :, 1]
        slast = p.inc("scalar", lambda e: e.activation(
            rstd[:, :], var_view, AF.Sqrt, bias=epsb[:, :]), "SE")
        p.wait("vector", "SE", slast)
        p.emit("vector", lambda e: e.reciprocal(rstd[:, :], rstd[:, :]))
        vsp()
        mean_view = mv[:, :].rearrange("p (k two) -> p k two", two=2)[:, :, 0]
        p.emit("vector", lambda e: e.tensor_tensor(nmr[:, :], mean_view, rstd[:, :], ALU.mult))
        vsp()
        return p.inc("vector", lambda e: e.tensor_scalar_mul(nmr[:, :], nmr[:, :], -1.0), "VE")

    def cross_core_max(idx, local_col):
        d1 = dma("dS", "sync", ccin[idx].rearrange("(p one) -> p one", one=1), local_col)
        p.wait("gpsimd", "dS", d1)
        cc = p.inc("gpsimd", lambda e, i=idx: e.collective_compute(
            "AllReduce", ALU.max, replica_groups=[list(range(8))],
            ins=[ccin[i].opt()], outs=[ccout[i].opt()]), "CC")
        p.wait("sync", "CC", cc)
        d2 = dma("dS", "sync", rowb[0:1, :], ccout[idx][None, :])
        p.wait("vector", "dS", d2)
        return p.inc("vector", lambda e, i=idx: e.tensor_reduce(
            sca[0:1, i:i + 1], rowb[0:1, :], AX.X, ALU.max), "VE")

    def make_scales(idx, sig_col, s_col, gve):
        p.wait("vector", "VE", gve)
        p.emit("vector", lambda e, i=idx, sc=sig_col: e.tensor_scalar(
            sca[0:1, 4:5], sca[0:1, i:i + 1], sig2[0:1, sc:sc + 1],
            1.0 / QMAX, ALU.mult, ALU.mult))
        p.emit("vector", lambda e: e.tensor_copy(sca[0:1, 6:7], epsb[0:1, 0:1]))
        p.emit("vector", lambda e: e.tensor_copy(sca[0:1, 7:8], epsb[0:1, 0:1]))
        ve = p.inc("vector", lambda e: e.reciprocal(sca[0:1, 5:6], sca[0:1, 4:5]), "VE")
        p.wait("sync", "VE", ve)
        dma("dS", "sync", bb[None, :], sca[0:1, 4:6])
        d2 = dma("dS", "sync", bc[:, s_col:s_col + 2],
                 bb[None, :].partition_broadcast(128))
        p.wait("vector", "dS", d2)
        return p.inc("vector", lambda e: e.engine_nop(), "VE")

    def emit_ymax_and_barrier(idx, sig_col, s_col):
        re_of = [p.cnt["VE"], p.cnt["VE"]]
        for k in range(NT):
            w = TW[k]
            b = k % 2
            p.wait("scalar", "VE", re_of[b])
            se = p.inc("scalar", lambda e, k=k, w=w, b=b: e.activation(
                r2t[b][0:w, :], x_tile(k), AF.Identity,
                bias=nmr[0:w, k:k + 1], scale=rstd[0:w, k:k + 1]), "SE")
            p.wait("vector", "SE", se)
            re_of[b] = p.inc("vector", lambda e, k=k, w=w, b=b: e.tensor_reduce(
                ymaxc[0:w, k:k + 1], r2t[b][0:w, :], AX.X, ALU.max,
                apply_absolute_value=True), "VE")
        p.wait("vector", "dS")
        ve = p.inc("vector", lambda e: e.tensor_reduce(
            rmx[:, :], ymaxc[:, :], AX.X, ALU.max), "VE")
        p.wait("sync", "VE", ve)
        gve = cross_core_max(idx, rmx[:, 0:1])
        return make_scales(idx, sig_col, s_col, gve)

    def emit_quant_lnq(s_col):
        vsp()
        p.emit("vector", lambda e, c=s_col: e.tensor_scalar_mul(
            rs1[:, :], rstd[:, :], bc[:, c + 1:c + 2]))
        p.emit("vector", lambda e, c=s_col: e.tensor_scalar_mul(
            bia1[:, :], nmr[:, :], bc[:, c + 1:c + 2]))
        vsp()
        v0 = p.inc("vector", lambda e: e.engine_nop(), "VE")
        pe_of = [0, 0]
        r2_of = [v0, v0]
        for k in range(NT):
            w = TW[k]
            b = k % 2
            p.wait("scalar", "VE", r2_of[b])
            se = p.inc("scalar", lambda e, k=k, w=w, b=b: e.activation(
                r2t[b][0:w, :], x_tile(k), AF.Identity,
                bias=bia1[0:w, k:k + 1], scale=rs1[0:w, k:k + 1]), "SE")
            p.wait("vector", "SE", se)
            p.wait("vector", "PE", pe_of[b])
            p.emit("vector", lambda e, w=w, b=b: e.tensor_scalar(
                r2t[b][0:w, :], r2t[b][0:w, :], C23, CLO, ALU.add, ALU.max))
            p.emit("vector", lambda e, w=w, b=b: e.tensor_scalar(
                qint[b][0:w, :], r2t[b][0:w, :], CHI, -C23, ALU.min, ALU.add))
            vq = p.inc("vector", lambda e, w=w, b=b: e.tensor_copy(
                qtm[b][0:w, :], qint[b][0:w, :]), "VE")
            r2_of[b] = vq
            p.wait("tensor", "VE", vq)
            p.wait("tensor", "SE")
            for hc in range(NHC):
                fn = lambda e, hc=hc, w=w, b=b: e.transpose(
                    ps_tr[:, hc * 128:hc * 128 + w],
                    qtm[b][0:w, hc * 128:(hc + 1) * 128], eye[0:w, 0:w])
                if hc == NHC - 1:
                    pe = p.inc("tensor", fn, "PE")
                else:
                    p.emit("tensor", fn)
            pe_of[b] = pe
            p.wait("scalar", "PE", pe)
            p.inc("scalar", lambda e, k=k, w=w: e.copy(
                actT3[:, :, k * 128:k * 128 + w], ps_tr_view(w)), "SE")
        return p.cnt["SE"]

    def emit_wq_quant(ic, b, lgb, lub, state):
        outs = []
        for wi, (land, sc_dst) in enumerate(((lgb, sga), (lub, sua))):
            p.wait("vector", "dS", state["dma"][ic][wi])
            p.wait("vector", "SE")
            p.emit("vector", lambda e, land=land: e.tensor_reduce(
                rmx[:, :], land[:, :], AX.X, ALU.max, apply_absolute_value=True))
            vsp()
            p.emit("vector", lambda e, sc_dst=sc_dst, ic=ic: e.tensor_scalar_mul(
                sc_dst[:, ic:ic + 1], rmx[:, :], 1.0 / QMAX))
            p.emit("vector", lambda e: e.reciprocal(invq[:, :], rmx[:, :]))
            vsp()
            ve = p.inc("vector", lambda e: e.tensor_scalar_mul(
                invq[:, :], invq[:, :], QMAX), "VE")
            p.wait("scalar", "VE", ve)
            p.emit("scalar", lambda e, land=land: e.activation(
                land[:, :], land[:, :], AF.Identity, bias=cmagb[:, :], scale=invq[:, :]))
            qb = state["qb"]
            wq_b = state["wq_bft"][qb]
            p.wait("scalar", "PE", state["bft_pe"][qb])
            p.emit("scalar", lambda e, land=land, qb=qb: e.activation(
                qint[qb][:, :], land[:, :], AF.Identity, bias=ncmagb[:, :]))
            ge = p.inc("scalar", lambda e, wq_b=wq_b, qb=qb: e.copy(
                wq_b[:, :], qint[qb][:, :]), "SE")
            state["qb"] = 1 - qb
            outs.append((qb, ge))
        state["land_free"][b] = outs[1][1]
        state["quant"][ic] = outs

    def emit_wq_transpose(ic, b, stages, state):
        ready = []
        for wi, scratch in enumerate((wgTsc, wuTsc)):
            qb, ge = state["quant"][ic][wi]
            wq_b = state["wq_bft"][qb]
            p.wait("tensor", "SE", ge)
            p.wait("tensor", "SE")
            for hc in range(NHC):
                fn = lambda e, hc=hc, wq_b=wq_b: e.transpose(
                    ps_tr[:, hc * 128:(hc + 1) * 128],
                    wq_b[:, hc * 128:(hc + 1) * 128], eye[:, :])
                if hc == NHC - 1:
                    pe = p.inc("tensor", fn, "PE")
                else:
                    p.emit("tensor", fn)
            state["bft_pe"][qb] = pe
            stage = stages[wi][b]
            p.wait("scalar", "PE", state["stage_pe"][wi][b])
            p.wait("scalar", "dS", state["stage_dma"][wi][b])
            p.wait("scalar", "PE", pe)
            se2 = p.inc("scalar", lambda e, stage=stage: e.copy(stage[:, :], ps_tr[:, :]), "SE")
            p.wait("sync", "SE", se2)
            state["stage_dma"][wi][b] = dma("dS", "sync", scratch[ic], stage[:, :])
            ready.append(se2)
        state["ready"][ic] = (ready[0], ready[1], None)

    def emit_gateup(call, se_actT_done, bufs, s_col, wd_hook=None, wd_post=None,
                    hb_hook=None):
        lgb, lub = bufs.get("lg"), bufs.get("lu")
        h_stage = bufs["h_stage"]
        stages = (bufs["wgT"], bufs["wuT"])
        state = {
            "qb": 0,
            "wq_bft": bufs.get("wq_bft"),
            "bft_pe": [0, 0],
            "stage_pe": [[0, 0], [0, 0]],
            "stage_dma": [[0, 0], [0, 0]],
            "land_free": [0, 0],
            "dma": {},
            "ready": {},
            "quant": {},
        }
        evac_ve = [0, 0]
        silu_se = [0, 0]
        hstore = [0, 0]
        prev_mm = 0
        if call == 1:
            p.wait("vector", "VE")
            p.emit("vector", lambda e, c=s_col: e.tensor_scalar_mul(
                sg1[:, :], sga[:, :], bc[:, c:c + 1]))
            p.inc("vector", lambda e, c=s_col: e.tensor_scalar_mul(
                su1[:, :], sua[:, :], bc[:, c:c + 1]), "VE")

        def weight_quant(ic):
            b = ic % 2
            if call == 0:
                if ic >= 2:
                    p.wait("gpsimd", "SE", state["land_free"][b])
                d0 = dma("dG", "gpsimd", lgb[b][:, :], wg_h[ic * 128:(ic + 1) * 128, :])
                d1 = dma("dG", "gpsimd", lub[b][:, :], wu_h[ic * 128:(ic + 1) * 128, :])
                state["dma"][ic] = (d0, d1)
                emit_wq_quant(ic, b, lgb[b], lub[b], state)
                p.emit("vector", lambda e, ic=ic, c=s_col: e.tensor_scalar_mul(
                    sg1[:, ic:ic + 1], sga[:, ic:ic + 1], bc[:, c:c + 1]))
                p.inc("vector", lambda e, ic=ic, c=s_col: e.tensor_scalar_mul(
                    su1[:, ic:ic + 1], sua[:, ic:ic + 1], bc[:, c:c + 1]), "VE")
            else:
                if ic >= 2:
                    p.wait("sync", "PE", state["stage_pe"][1][b])
                dma("dS", "sync", stages[0][b][:, :], wgTsc[ic])
                du = dma("dS", "sync", stages[1][b][:, :], wuTsc[ic])
                state["ready"][ic] = (None, None, du)

        def weight_fin(ic):
            if call == 0:
                emit_wq_transpose(ic, ic % 2, stages, state)

        state["quant"] = {}
        weight_quant(0)
        weight_fin(0)
        weight_quant(1)
        for ic in range(NIC):
            b = ic % 2
            if ic + 1 < NIC:
                weight_fin(ic + 1)
            if call == 0 and ic + 2 < NIC:
                weight_quant(ic + 2)
            if wd_hook is not None:
                wd_hook(ic)
            rg, ru, du = state["ready"][ic]
            if call == 0:
                p.wait("tensor", "SE", rg)
                p.wait("tensor", "SE", ru)
            else:
                p.wait("tensor", "dS", du)
            p.wait("tensor", "SE", se_actT_done)
            for it, (t0, tw_) in enumerate(TCH):
                pb = it % 2
                p.wait("tensor", "VE", evac_ve[pb])
                p.wait("tensor", "SE", silu_se[pb])
                for pi, (pst, wst) in enumerate(((ps_a, stages[0][b]), (ps_b, stages[1][b]))):
                    for hc in range(NHC):
                        fn = lambda e, hc=hc, t0=t0, tw_=tw_, pst=pst, wst=wst, pb=pb: e.matmul(
                            pst[pb][:, 0:tw_], wst[:, hc * 128:(hc + 1) * 128],
                            actT[:, hc * TR + t0:hc * TR + t0 + tw_],
                            start=(hc == 0), stop=(hc == NHC - 1))
                        if pi == 1 and hc == NHC - 1:
                            prev_mm = p.inc("tensor", fn, "PE")
                        else:
                            p.emit("tensor", fn)
                p.wait("scalar", "PE", prev_mm)
                p.wait("scalar", "VE")
                silu_se[pb] = p.inc("scalar", lambda e, ic=ic, tw_=tw_, pb=pb: e.activation(
                    silu_t[pb][:, 0:tw_], ps_a[pb][:, 0:tw_], AF.Silu,
                    scale=sg1[:, ic:ic + 1]), "SE")
                hb = ic % 2
                p.wait("vector", "SE", silu_se[pb])
                p.wait("vector", "PE", prev_mm)
                if it == 0:
                    p.wait("vector", "dS", hstore[hb])
                evac_ve[pb] = p.inc("vector", lambda e, ic=ic, t0=t0, tw_=tw_, pb=pb, hb=hb: e.scalar_tensor_tensor(
                    h_stage[hb][:, t0:t0 + tw_], ps_b[pb][:, 0:tw_], su1[:, ic:ic + 1],
                    silu_t[pb][:, 0:tw_], ALU.mult, ALU.mult), "VE")
            vsp()
            p.emit("vector", lambda e, ic=ic, hb=hb: e.tensor_reduce(
                hmaxc[:, ic:ic + 1], h_stage[hb][:, 0:TR],
                AX.X, ALU.max, apply_absolute_value=True))
            state["stage_pe"][0][b] = prev_mm
            state["stage_pe"][1][b] = prev_mm
            if call == 1 and ic + 2 < NIC:
                weight_quant(ic + 2)
            if hb_hook is not None and ic == NIC - 1:
                hb_hook()
            if wd_post is not None:
                wd_post(ic)
            p.wait("sync", "VE", evac_ve[(len(TCH) - 1) % 2])
            p.wait("sync", "VE")
            hstore[ic % 2] = dma("dS", "sync", hsc[ic], h_stage[ic % 2][:, :])
        return prev_mm

    def emit_hbarrier_start(idx):
        p.wait("vector", "dS")
        vsp()
        ve = p.inc("vector", lambda e: e.tensor_reduce(
            rmx[:, :], hmaxc[:, :], AX.X, ALU.max), "VE")
        p.wait("sync", "VE", ve)
        d1 = dma("dS", "sync", ccin[idx].rearrange("(p one) -> p one", one=1),
                 rmx[:, 0:1])
        p.wait("gpsimd", "dS", d1)
        return p.inc("gpsimd", lambda e, i=idx: e.collective_compute(
            "AllReduce", ALU.max, replica_groups=[list(range(8))],
            ins=[ccin[i].opt()], outs=[ccout[i].opt()]), "CC")

    def emit_hbarrier_fin(idx, s_col, cc):
        p.wait("sync", "CC", cc)
        d2 = dma("dS", "sync", rowb[0:1, :], ccout[idx][None, :])
        p.wait("vector", "dS", d2)
        gve = p.inc("vector", lambda e, i=idx: e.tensor_reduce(
            sca[0:1, i:i + 1], rowb[0:1, :], AX.X, ALU.max), "VE")
        return make_scales(idx, 1, s_col, gve)

    def down_gather(hqr2b, k, w):
        hv = hqr2b[:, :].rearrange("p (c t) -> p c t", c=NIC)
        NH1 = NIC // 2
        return [dma("dG", "gpsimd", hv[:, h * NH1:(h + 1) * NH1, 0:w],
                    hsc[h * NH1:(h + 1) * NH1, :, k * 128:k * 128 + w]
                    .rearrange("c p t -> p c t")) for h in range(2)]

    def emit_down(s_col, ve_sc, wdT, bufs, bn_hook=None, pre_g=None, wdTb=0,
                  out_hook=None):
        hq_bf, hqr2, hqi8 = bufs["hq_bf"], bufs["hqr2"], bufs["hqi8"]
        p.wait("gpsimd", "dS")
        p.wait("vector", "VE", ve_sc)
        p.wait("scalar", "VE", ve_sc)
        pe_down = [0, 0]
        vq_of = [0, 0]
        vprev = [0, 0]
        NH1 = NIC // 2
        for k in range(NT):
            w = TW[k]
            b = k % 2
            if k == 0 and pre_g is not None:
                dkh = pre_g
            else:
                if k >= 2:
                    p.wait("gpsimd", "VE", vq_of[b])
                dkh = down_gather(hqr2[b], k, w)
            vqh = [0, 0]
            for h in range(2):
                c0 = h * NH1 * 128
                cw = NH1 * 128
                p.wait("scalar", "dG", dkh[h])
                seh = p.inc("scalar", lambda e, b=b, c0=c0, cw=cw: e.activation(
                    hqr2[b][:, c0:c0 + cw], hqr2[b][:, c0:c0 + cw], AF.Identity,
                    bias=cmagb[:, :], scale=bc[:, s_col + 1:s_col + 2]), "SE")
                p.wait("vector", "SE", seh)
                if h == 0:
                    p.wait("vector", "PE", pe_down[b])
                p.emit("vector", lambda e, b=b, c0=c0, cw=cw: e.tensor_scalar(
                    hqr2[b][:, c0:c0 + cw], hqr2[b][:, c0:c0 + cw],
                    CLO, CHI, ALU.max, ALU.min))
                p.emit("vector", lambda e, b=b, c0=c0, cw=cw: e.tensor_single_scalar(
                    hqi8[b][:, c0:c0 + cw], hqr2[b][:, c0:c0 + cw], -C23, ALU.add))
                vqh[h] = p.inc("vector", lambda e, b=b, c0=c0, cw=cw: e.tensor_copy(
                    hq_bf[b][:, c0:c0 + cw], hqi8[b][:, c0:c0 + cw]), "VE")
            vq_of[b] = vqh[1]
            p.wait("tensor", "VE", vqh[0])
            p.wait("tensor", "VE", vprev[0])
            p.wait("tensor", "VE", vprev[1])
            for e2 in range(2):
                for ic in range(NIC):
                    if k == 0 and e2 == 0 and ic == NH1 and wdTb:
                        p.wait("tensor", "dG", wdTb)
                    if e2 == 0 and ic == NH1:
                        p.wait("tensor", "VE", vqh[1])
                    fn = lambda e, ic=ic, e2=e2, b=b, w=w: e.matmul(
                        ps_c[e2][0:w, 0:512], hq_bf[b][:, ic * 128:ic * 128 + w],
                        wdT[:, ic * D + e2 * 512:ic * D + e2 * 512 + 512],
                        start=(ic == 0), stop=(ic == NIC - 1))
                    if e2 == 1 and ic == NIC - 1:
                        pe_down[b] = p.inc("tensor", fn, "PE")
                    else:
                        p.emit("tensor", fn)
            p.wait("vector", "PE", pe_down[b])
            for e2 in range(2):
                vprev[e2] = p.inc("vector", lambda e, e2=e2, k=k, w=w: e.scalar_tensor_tensor(
                    x_res[0:w, k * D + e2 * 512:k * D + e2 * 512 + 512],
                    ps_c[e2][0:w, 0:512], bc[0:w, s_col:s_col + 1],
                    x_res[0:w, k * D + e2 * 512:k * D + e2 * 512 + 512],
                    ALU.mult, ALU.add), "VE")
            if bn_hook is not None:
                bn_hook(k, w)
            if out_hook is not None:
                out_hook(k)
        return p.inc("vector", lambda e: e.engine_nop(), "VE")

    if DEBUG_STAGE == 10:
        return _finish(nc, p, dma, x_res, out_h, dump=(sca, bc, rowb, hmaxc, wgTsc, wuTsc, hsc, sg1, su1, sga))
    emit_ln()
    if DEBUG_STAGE == 11:
        return _finish(nc, p, dma, x_res, out_h, dump=(sca, bc, rowb, hmaxc, wgTsc, wuTsc, hsc, sg1, su1, sga))
    sc1 = emit_ymax_and_barrier(0, 0, 0)
    if DEBUG_STAGE == 12:
        return _finish(nc, p, dma, x_res, out_h, dump=(sca, bc, rowb, hmaxc, wgTsc, wuTsc, hsc, sg1, su1, sga))
    p.wait("vector", "VE", sc1)
    se_actT = emit_quant_lnq(0)
    if DEBUG_STAGE == 13:
        return _finish(nc, p, dma, x_res, out_h, dump=(sca, bc, rowb, hmaxc, wgTsc, wuTsc, hsc, sg1, su1, sga))

    s1ctx = ExitStack()
    bufs1 = {
        "lg": [sb(f"lg{i}", [128, D], F32, s1ctx) for i in range(2)],
        "lu": [sb(f"lu{i}", [128, D], F32, s1ctx) for i in range(2)],
        "wgT": [sb(f"wgT{i}", [128, D], BF16, s1ctx) for i in range(2)],
        "wuT": [sb(f"wuT{i}", [128, D], BF16, s1ctx) for i in range(2)],
        "wq_bft": [sb(f"wq_bft{i}", [128, D], BF16, s1ctx) for i in range(2)],
    }
    wdct = ExitStack()
    wd_land = [[sb(f"wdl{b}_{i}", [128, I // 2], F32, wdct) for i in range(2)]
               for b in range(2)]
    wdinvq = [sb(f"wdinvq{b}", [128, 1], F32, wdct) for b in range(2)]
    sdh2 = [sb(f"sdh2_{b}", [128, 1], F32, wdct) for b in range(2)]
    wdi8 = sb("wdi8", [128, I // 4], INT8, wdct)
    wd_q = [sb(f"wd_q{i}", [128, I], BF16, wdct) for i in range(2)]
    wd_stage = sb("wd_stage", [128, (NIC // 2) * 128], BF16, wdct)
    HI = I // 2
    HQ4 = I // 4
    wd_state = {"ge_b": [0, 0], "pe": 0, "pe_b": [0, 0], "sdma": 0,
                "halfq": [0, 0], "ve127_b": [0, 0], "act_se": [0, 0]}
    hstctx = ExitStack()
    bufs1["h_stage"] = [sb(f"hst{i}", [128, TR], F16, hstctx) for i in range(2)]
    wdTsc3 = wdTsc[:, :].rearrange("p (c d) -> p c d", c=NIC)

    def emit_wd_part1(hc):
        qb = hc % 2
        land = wd_land[qb]
        p.wait("gpsimd", "VE", wd_state["ge_b"][qb])
        d0 = dma("dG", "gpsimd", land[0][:, :], wd_h[hc * 128:(hc + 1) * 128, 0:HI])
        d1 = dma("dG", "gpsimd", land[1][:, :], wd_h[hc * 128:(hc + 1) * 128, HI:I])
        p.wait("vector", "dG", d1)
        p.wait("vector", "SE", wd_state["act_se"][qb])
        p.emit("vector", lambda e, land=land: e.tensor_reduce(
            rmx[:, :], land[0][:, :], AX.X, ALU.max, apply_absolute_value=True))
        p.emit("vector", lambda e, land=land, qb=qb: e.tensor_reduce(
            wdinvq[qb][:, :], land[1][:, :], AX.X, ALU.max, apply_absolute_value=True))
        vsp()
        p.emit("vector", lambda e, qb=qb: e.tensor_tensor(
            rmx[:, :], rmx[:, :], wdinvq[qb][:, :], ALU.max))
        vsp()
        p.emit("vector", lambda e, qb=qb: e.tensor_scalar_mul(
            sdh2[qb][:, :], rmx[:, :], 0.5 / QMAX))
        p.emit("vector", lambda e, qb=qb: e.reciprocal(wdinvq[qb][:, :], rmx[:, :]))
        vsp()
        wd_state["ve127_b"][qb] = p.inc("vector", lambda e, qb=qb: e.tensor_scalar_mul(
            wdinvq[qb][:, :], wdinvq[qb][:, :], QMAX), "VE")

    def emit_wd_part2(hc):
        qb = hc % 2
        land = wd_land[qb]
        p.wait("scalar", "VE", wd_state["ve127_b"][qb])
        for half in range(2):
            se = p.inc("scalar", lambda e, half=half, land=land, qb=qb: e.activation(
                land[half][:, :], land[half][:, :], AF.Identity,
                bias=cmagb[:, :], scale=wdinvq[qb][:, :]), "SE")
            p.wait("vector", "SE", se)
            if half == 0:
                p.wait("vector", "PE", wd_state["pe_b"][qb])
            for q in range(2):
                p.emit("vector", lambda e, half=half, q=q, land=land: e.tensor_single_scalar(
                    wdi8[:, :], land[half][:, q * HQ4:(q + 1) * HQ4], -C23, ALU.add))
                wd_state["ge_b"][qb] = p.inc("vector", lambda e, half=half, q=q, qb=qb: e.tensor_scalar_mul(
                    wd_q[qb][:, half * HI + q * HQ4:half * HI + (q + 1) * HQ4],
                    wdi8[:, :], sdh2[qb][:, :]), "VE")
            wd_state["halfq"][half] = wd_state["ge_b"][qb]
        wd_state["act_se"][qb] = p.cnt["SE"]

    def emit_wd_transpose(hc):
        for half in range(2):
            p.wait("tensor", "VE", wd_state["halfq"][half])
            nloc = NIC // 2
            p.wait("tensor", "SE")
            p.wait("tensor", "dS", wd_state["sdma"])
            for gstart in range(0, nloc, 8):
                n = min(8, nloc - gstart)
                p.wait("tensor", "SE")
                for j in range(n):
                    lic = gstart + j
                    fn = lambda e, lic=lic, j=j, half=half, hc=hc: e.transpose(
                        ps_tr[:, j * 128:(j + 1) * 128],
                        wd_q[hc % 2][:, half * HI + lic * 128:half * HI + (lic + 1) * 128],
                        eye[:, :])
                    if j == n - 1:
                        wd_state["pe"] = p.inc("tensor", fn, "PE")
                        wd_state["pe_b"][hc % 2] = wd_state["pe"]
                    else:
                        p.emit("tensor", fn)
                p.wait("scalar", "PE", wd_state["pe"])
                p.inc("scalar", lambda e, gstart=gstart, n=n: e.copy(
                    wd_stage[:, gstart * 128:(gstart + n) * 128],
                    ps_tr[:, 0:n * 128]), "SE")
            se = p.cnt["SE"]
            p.wait("sync", "SE", se)
            g0 = half * nloc
            wd_state["sdma"] = dma(
                "dS", "sync",
                wdTsc3[:, g0:g0 + nloc, hc * 128:(hc + 1) * 128],
                wd_stage[:, :].rearrange("p (c t) -> p c t", c=nloc))

    def wd_hook(ic):
        if ic % 3 == 0 and ic // 3 < NHC - 1:
            emit_wd_part1(ic // 3)
        if ic == 19:
            emit_wd_part1(7)
        if ic % 3 == 0 and ic >= 3:
            emit_wd_transpose((ic - 3) // 3)

    def wd_post(ic):
        if ic % 3 == 0 and ic // 3 < NHC - 1:
            emit_wd_part2(ic // 3)
        if ic == 19:
            emit_wd_part2(7)
        if ic == NIC - 1:
            emit_wd_transpose(7)

    mm1_end = emit_gateup(0, se_actT, bufs1, 0, wd_hook=wd_hook, wd_post=wd_post)
    hstctx.close()
    wdct.close()
    s1ctx.close()
    wdT_ctx = ExitStack()
    wdT = sb("wdT", [128, NIC * D], BF16, wdT_ctx)
    dwn1 = ExitStack()
    bufs_d1 = {
        "hq_bf": [sb(f"hqb{i}", [128, NIC * 128], BF16, dwn1) for i in range(2)],
        "hqr2": [sb(f"hqr2f{i}", [128, NIC * 128], F32, dwn1) for i in range(2)],
        "hqi8": [sb("hqi8f0", [128, NIC * 128], INT8, dwn1)] * 2,
    }
    p.wait("gpsimd", "PE", wd_state["pe"])
    p.wait("gpsimd", "VE")
    p.wait("gpsimd", "dS", wd_state["sdma"])
    HW1 = (NIC // 2) * D
    d_wdTa = dma("dG", "gpsimd", wdT[:, 0:HW1], wdTsc[:, 0:HW1])
    p.wait("gpsimd", "dS")
    g0 = down_gather(bufs_d1["hqr2"][0], 0, TW[0])
    d_wdTb = dma("dG", "gpsimd", wdT[:, HW1:NIC * D], wdTsc[:, HW1:NIC * D])
    cc2 = emit_hbarrier_start(1)
    sc2 = emit_hbarrier_fin(1, 2, cc2)
    p.wait("tensor", "dG", d_wdTa)
    emit_down(2, sc2, wdT, bufs_d1, bn_hook=emit_bn, pre_g=g0, wdTb=d_wdTb)

    phase_waits()
    dwn1.close()
    wdT_ctx.close()

    if DEBUG_STAGE == 1:
        return _finish(nc, p, dma, x_res, out_h, dump=(sca, bc, rowb, hmaxc, wgTsc, wuTsc, hsc, sg1, su1, sga))

    att1 = ExitStack()
    W1sb = sb("W1sb", [128, NHC * E1], BF16, att1)
    b2r = sb("b2r", [1, D], BF16, att1)
    dma("dG", "gpsimd", b2r[0:1, :], b2_h[None, :])
    diag2 = sb("diag2", [128, KCONV * 128], BF16, att1)
    att2 = ExitStack()
    GPW = TR + 2 * HALO
    glup = sb("glup", [128, NHC * GPW], BF16, att2)
    sig_t = silu_t

    def diag_ap(db, k):
        if db == 0:
            if k < 8:
                return qtm[0][:, k * 128:(k + 1) * 128]
            return qtm[1][:, (k - 8) * 128:(k - 7) * 128]
        return diag2[:, k * 128:(k + 1) * 128]

    for dc in range(NHC):
        dma("dG", "gpsimd", W1sb[:, dc * E1:(dc + 1) * E1], w1_h[dc * 128:(dc + 1) * 128, :])

    emit_ln_tail()
    pe_of = [0, 0]
    for k in range(NT):
        w = TW[k]
        b = k % 2
        p.wait("scalar", "PE", pe_of[b])
        p.wait("scalar", "VE")
        se = p.inc("scalar", lambda e, k=k, w=w, b=b: e.activation(
            qtm[b][0:w, :], x_tile(k), AF.Identity,
            bias=nmr[0:w, k:k + 1], scale=rstd[0:w, k:k + 1]), "SE")
        p.wait("tensor", "SE", se)
        p.wait("tensor", "SE")
        for hc in range(NHC):
            fn = lambda e, hc=hc, w=w, b=b: e.transpose(
                ps_tr[:, hc * 128:hc * 128 + w], qtm[b][0:w, hc * 128:(hc + 1) * 128], eye[0:w, 0:w])
            if hc == NHC - 1:
                pe = p.inc("tensor", fn, "PE")
            else:
                p.emit("tensor", fn)
        pe_of[b] = pe
        p.wait("scalar", "PE", pe)
        p.inc("scalar", lambda e, k=k, w=w: e.copy(
            actT3[:, :, k * 128:k * 128 + w], ps_tr_view(w)), "SE")
    se_y2T = p.cnt["SE"]

    for dc in range(NHC):
        p.emit("vector", lambda e, dc=dc: e.memset(glup[:, dc * GPW:dc * GPW + HALO], 0.0))
        p.emit("vector", lambda e, dc=dc: e.memset(
            glup[:, dc * GPW + HALO + TR:(dc + 1) * GPW], 0.0))
    ve_gmem = p.inc("vector", lambda e: e.engine_nop(), "VE")

    p.wait("tensor", "SE", se_y2T)
    p.wait("tensor", "dG")
    glu_ve_b = [0, 0]
    sig_se = [0, 0]
    for ec in range(NHC):
        for it, (t0, tw_) in enumerate(TCH):
            pb = it % 2
            p.wait("tensor", "VE", glu_ve_b[pb])
            p.wait("tensor", "SE", sig_se[pb])
            for half, pst in ((0, ps_a), (1, ps_b)):
                ecc = ec + half * NHC
                for dc in range(NHC):
                    fn = lambda e, dc=dc, ecc=ecc, t0=t0, tw_=tw_, pst=pst, pb=pb: e.matmul(
                        pst[pb][:, 0:tw_],
                        W1sb[:, dc * E1 + ecc * 128:dc * E1 + (ecc + 1) * 128],
                        actT[:, dc * TR + t0:dc * TR + t0 + tw_],
                        start=(dc == 0), stop=(dc == NHC - 1))
                    if half == 1 and dc == NHC - 1:
                        pe = p.inc("tensor", fn, "PE")
                    else:
                        p.emit("tensor", fn)
            p.wait("scalar", "PE", pe)
            p.wait("scalar", "VE")
            sig_se[pb] = p.inc("scalar", lambda e, ec=ec, tw_=tw_, pb=pb: e.activation(
                sig_t[pb][:, 0:tw_], ps_b[pb][:, 0:tw_], AF.Sigmoid,
                bias=b1c[:, NHC + ec:NHC + ec + 1]), "SE")
            p.wait("vector", "SE", sig_se[pb])
            p.wait("vector", "VE", ve_gmem)
            glu_ve_b[pb] = p.inc("vector", lambda e, ec=ec, t0=t0, tw_=tw_, pb=pb: e.scalar_tensor_tensor(
                glup[:, ec * GPW + HALO + t0:ec * GPW + HALO + t0 + tw_],
                ps_a[pb][:, 0:tw_], b1c[:, ec:ec + 1], sig_t[pb][:, 0:tw_],
                ALU.add, ALU.mult), "VE")
    p.wait("vector", "VE", max(glu_ve_b))
    for dc in range(NHC):
        p.emit("vector", lambda e, dc=dc: e.tensor_scalar_mul(
            glup[:, dc * GPW + HALO:dc * GPW + 2 * HALO],
            glup[:, dc * GPW + HALO:dc * GPW + 2 * HALO], edge[:, 0:1]))
        p.emit("vector", lambda e, dc=dc: e.tensor_scalar_mul(
            glup[:, dc * GPW + HALO + TR - HALO:dc * GPW + HALO + TR],
            glup[:, dc * GPW + HALO + TR - HALO:dc * GPW + HALO + TR], edge[:, 1:2]))
    ve_glu = p.inc("vector", lambda e: e.engine_nop(), "VE")

    convout = actT
    convout3 = actT3
    conv_evac = [0, 0]
    pe_conv_b = [0, 0]
    diag_ve = [0, 0]

    def build_diag(dc):
        db = dc % 2
        p.wait("vector", "PE", pe_conv_b[db])
        for k in range(KCONV):
            fn = lambda e, dc=dc, k=k, db=db: e.tensor_scalar_mul(
                diag_ap(db, k), eye[:, :],
                filt[:, dc * KCONV + k:dc * KCONV + k + 1])
            p.emit("vector", fn)
        diag_ve[db] = p.inc("vector", lambda e: e.engine_nop(), "VE")

    p.wait("vector", "VE", ve_glu)
    build_diag(0)
    for dc in range(NHC):
        db = dc % 2
        if dc + 1 < NHC:
            build_diag(dc + 1)
        p.wait("tensor", "VE", diag_ve[db])
        for it, (t0, tw_) in enumerate(TCH):
            pb = it % 2
            p.wait("tensor", "SE", conv_evac[pb])
            for k in range(KCONV):
                fn = lambda e, dc=dc, k=k, t0=t0, tw_=tw_, pb=pb, db=db: e.matmul(
                    ps_c[pb][:, 0:tw_], diag_ap(db, k),
                    glup[:, dc * GPW + t0 + k:dc * GPW + t0 + k + tw_],
                    start=(k == 0), stop=(k == KCONV - 1))
                if k == KCONV - 1:
                    pe_conv_b[db] = p.inc("tensor", fn, "PE")
                else:
                    p.emit("tensor", fn)
            p.wait("scalar", "PE", pe_conv_b[db])
            conv_evac[pb] = p.inc("scalar", lambda e, dc=dc, t0=t0, tw_=tw_, pb=pb: e.copy(
                convout[:, dc * TR + t0:dc * TR + t0 + tw_], ps_c[pb][:, 0:tw_]), "SE")

    se_conv = p.cnt["SE"]
    p.wait("gpsimd", "PE")
    p.wait("gpsimd", "SE", se_conv)
    att2.close()
    att3 = ExitStack()
    W2sb = sb("W2sb", [128, NHC * D], BF16, att3)
    dW2 = 0
    for dc in range(NHC):
        dW2 = dma("dG", "gpsimd", W2sb[:, dc * D:(dc + 1) * D], w2_h[dc * 128:(dc + 1) * 128, :])
    p.wait("tensor", "SE", se_conv)
    p.wait("tensor", "dG", dW2)
    w2_ve = [0, 0]
    for k in range(NT):
        w = TW[k]
        for e2 in range(2):
            p.wait("tensor", "VE", w2_ve[e2])
            for dc in range(NHC):
                fn = lambda e, dc=dc, e2=e2, k=k, w=w: e.matmul(
                    ps_c[e2][0:w, 0:512],
                    convout[:, dc * TR + k * 128:dc * TR + k * 128 + w],
                    W2sb[:, dc * D + e2 * 512:dc * D + e2 * 512 + 512],
                    start=(dc == 0), stop=False)
                p.emit("tensor", fn)
            fn = lambda e, e2=e2, k=k, w=w: e.matmul(
                ps_c[e2][0:w, 0:512], onesb[0:1, 0:w], b2r[0:1, e2 * 512:e2 * 512 + 512],
                start=False, stop=True)
            pe = p.inc("tensor", fn, "PE")
            p.wait("vector", "PE", pe)
            w2_ve[e2] = p.inc("vector", lambda e, e2=e2, k=k, w=w: e.tensor_tensor(
                x_res[0:w, k * D + e2 * 512:k * D + e2 * 512 + 512],
                ps_c[e2][0:w, 0:512],
                x_res[0:w, k * D + e2 * 512:k * D + e2 * 512 + 512], ALU.add), "VE")
        if k == 0:
            p.emit("vector", lambda e: e.memset(x_res[0:HALO, 0:D], 0.0))
        if k == NT - 1:
            p.wait("sync", "VE", w2_ve[1])
            dz = dma("dS", "sync", x_res[RAG - HALO:RAG, (NT - 1) * D:NT * D],
                     x_h[TPAD - HALO:TPAD, :])
            p.wait("vector", "dS", dz)
        emit_bn(k, w)
        p.emit("vector", lambda e, k=k, w=w: e.tensor_reduce(
            xmx[0:w, k:k + 1], x_res[0:w, k * D:(k + 1) * D], AX.X, ALU.max))
        p.emit("vector", lambda e, k=k, w=w: e.tensor_reduce(
            xmn[0:w, k:k + 1], x_res[0:w, k * D:(k + 1) * D], AX.X, ALU.min))

    phase_waits()
    att3.close()
    att1.close()

    if DEBUG_STAGE == 2:
        return _finish(nc, p, dma, x_res, out_h, dump=(sca, bc, rowb, hmaxc, wgTsc, wuTsc, hsc, sg1, su1, sga))

    emit_ln_tail()
    mean_v = mv[:, :].rearrange("p (k two) -> p k two", two=2)[:, :, 0]
    p.emit("vector", lambda e: e.tensor_tensor(
        xmx[:, :], xmx[:, :], mean_v, ALU.subtract))
    p.emit("vector", lambda e: e.tensor_tensor(
        xmn[:, :], mean_v, xmn[:, :], ALU.subtract))
    vsp()
    p.emit("vector", lambda e: e.tensor_tensor(xmx[:, :], xmx[:, :], xmn[:, :], ALU.max))
    p.emit("vector", lambda e: e.tensor_tensor(xmx[:, :], xmx[:, :], rstd[:, :], ALU.mult))
    vsp()
    p.wait("vector", "dS")
    ve3 = p.inc("vector", lambda e: e.tensor_reduce(
        rmx[:, :], xmx[:, :], AX.X, ALU.max), "VE")
    p.wait("sync", "VE", ve3)
    gve3 = cross_core_max(2, rmx[:, 0:1])
    sc3 = make_scales(2, 0, 0, gve3)
    p.wait("vector", "VE", sc3)
    se_actT3v = emit_quant_lnq(0)

    wdT_ctx2 = ExitStack()
    wdT2 = sb("wdT2", [128, NIC * D], BF16, wdT_ctx2)
    d_wdT = dma("dG", "gpsimd", wdT2[:, :], wdTsc[:, :])
    s2ctx = ExitStack()
    bufs2 = {
        "wgT": [sb(f"wgT2_{i}", [128, D], BF16, s2ctx) for i in range(2)],
        "h_stage": [sb(f"hst2_{i}", [128, TR], F16, s2ctx) for i in range(2)],
        "wuT": [sb(f"wuT2_{i}", [128, D], BF16, s2ctx) for i in range(2)],
    }

    mm2_end = emit_gateup(1, se_actT3v, bufs2, 0)
    cc4 = emit_hbarrier_start(3)
    sc4 = emit_hbarrier_fin(3, 2, cc4)
    p.wait("sync", "PE", mm2_end)
    s2ctx.close()

    dwn2 = ExitStack()
    bufs_d2 = {
        "hq_bf": [sb(f"hqb2_{i}", [128, NIC * 128], BF16, dwn2) for i in range(2)],
        "hqr2": [sb(f"hqr2g{i}", [128, NIC * 128], F32, dwn2) for i in range(2)],
        "hqi8": [sb("hqi8g0", [128, NIC * 128], INT8, dwn2)] * 2,
    }
    p.wait("tensor", "dG", d_wdT)

    def out_store(k):
        p.wait("sync", "VE")
        if k == 0:
            dma("dS", "sync", out_h[0:121, :], x_res[HALO:128, 0:D])
        elif k == NT - 1:
            dma("dS", "sync", out_h[128 * k - HALO:TOK, :],
                x_res[0:RAG - HALO, k * D:(k + 1) * D])
        else:
            dma("dS", "sync", out_h[128 * k - HALO:128 * (k + 1) - HALO, :],
                x_res[0:128, k * D:(k + 1) * D])

    ve_done = emit_down(2, sc4, wdT2, bufs_d2, out_hook=out_store)

    p.wait("sync", "VE", ve_done)
    nc2 = _finish(nc, p, dma, x_res, out_h,
                  dump=(sca, bc, rowb, hmaxc, wgTsc, wuTsc, hsc, sg1, su1, sga),
                  stores_done=True)
    dwn2.close()
    wdT_ctx2.close()
    ctx.close()
    return nc2


_NC_CACHE = None


def _get_nc():
    global _NC_CACHE
    if _NC_CACHE is None:
        _NC_CACHE = build_nc()
    return _NC_CACHE


def _stage_inputs(x, W1, b1, W2, b2, csw, clip_x, clip_down, wg, wu, wd):
    x = np.asarray(x, dtype=np.float32)
    eye = np.eye(128, dtype=ml_dtypes.bfloat16)
    onesf = np.ones((1, 128), dtype=np.float32)
    onesb = np.ones((1, 128), dtype=ml_dtypes.bfloat16)
    sel = np.zeros((16, NHC * 128), dtype=ml_dtypes.bfloat16)
    for dc in range(NHC):
        for pp in range(128):
            sel[2 * dc + pp // 64, dc * 128 + pp] = 1.0
    clips = np.array([[float(np.asarray(clip_x).ravel()[0]),
                       float(np.asarray(clip_down).ravel()[0])]], dtype=np.float32)
    common = {
        "w1": np.ascontiguousarray(np.asarray(W1, dtype=np.float32)),
        "b1": np.ascontiguousarray(np.asarray(b1, dtype=np.float32)),
        "w2": np.ascontiguousarray(np.asarray(W2, dtype=np.float32)),
        "b2": np.ascontiguousarray(np.asarray(b2, dtype=np.float32)),
        "csw": np.ascontiguousarray(np.asarray(csw, dtype=np.float32)),
        "clips": clips,
        "wg": np.ascontiguousarray(np.asarray(wg, dtype=np.float32)),
        "wu": np.ascontiguousarray(np.asarray(wu, dtype=np.float32)),
        "wd": np.ascontiguousarray(np.asarray(wd, dtype=np.float32)),
        "eyeI": eye,
        "onesf": onesf,
        "onesb": onesb,
        "sel": sel,
    }
    in_maps = []
    for c in range(8):
        b, s = c // 2, c % 2
        start = s * TOK - HALO
        xs = np.zeros((TPAD, D), dtype=np.float32)
        lo = max(start, 0)
        hi = min(start + TR, S)
        xs[lo - start:hi - start] = x[b, lo:hi]
        edgev = np.ones((128, 2), dtype=np.float32)
        edgev[:, 0 if s == 0 else 1] = 0.0
        m = dict(common)
        m["x"] = xs
        m["edge"] = edgev
        in_maps.append(m)
    return in_maps


def kernel(L0_attn_inputs, L0_attn_W1, L0_attn_b1, L0_attn_W2, L0_attn_b2,
           L0_attn_conv_shared_weight, L0_ffn_clip_x, L0_ffn_clip_down,
           L0_ffn_w_gate, L0_ffn_w_up, L0_ffn_w_down):
    nc = _get_nc()
    in_maps = _stage_inputs(L0_attn_inputs, L0_attn_W1, L0_attn_b1, L0_attn_W2,
                            L0_attn_b2, L0_attn_conv_shared_weight,
                            L0_ffn_clip_x, L0_ffn_clip_down,
                            L0_ffn_w_gate, L0_ffn_w_up, L0_ffn_w_down)
    res = run_bass_kernel_spmd(nc, in_maps, core_ids=list(range(8)))
    out = np.empty((B, S, D), dtype=np.float32)
    for c in range(8):
        b, s = c // 2, c % 2
        out[b, s * TOK:(s + 1) * TOK] = res.results[c]["out"]
    return out
